# version 06
# baseline (speedup 1.0000x reference)
import numpy as np
import ml_dtypes
from contextlib import ExitStack

import concourse.bass as bass
import concourse.mybir as mybir
from concourse.bass_utils import run_bass_kernel_spmd

F32 = mybir.dt.float32
BF16 = mybir.dt.bfloat16
AF = mybir.ActivationFunctionType
ALU = mybir.AluOpType
AX = mybir.AxisListType

P = 128
D = 1024
KD = D // P
E = 8
FF = 4096
FCH = FF // P
FGN = 4
FGS = FCH // FGN
NW1 = 10
NW2 = 10
LN_EPS = 1e-5
NORM_EPS = 1e-6
NCORES = 8
TQW = 512

ENGINE_ATTRS = {"t": "tensor", "v": "vector", "a": "scalar", "sy": "sync", "g": "gpsimd"}


class Ctx:
    def __init__(self, nc, es):
        self.nc = nc
        self.es = es
        self.sems = {}
        self.cnt = {}
        self.step = {}
        self.waited = {}
        self.tr_evac = [None, None]

    def sem(self, name, step=1):
        self.sems[name] = self.es.enter_context(self.nc.semaphore(name))
        self.cnt[name] = 0
        self.step[name] = step

    def inc(self, instr, s):
        self.cnt[s] += 1
        instr.then_inc(self.sems[s], self.step[s])
        return self.cnt[s]

    def dma(self, ekey, out, in_, s):
        ins = self.eng(ekey).dma_start(out, in_)
        ins.then_inc(self.sems[s], self.step[s])
        self.cnt[s] += 1
        return self.cnt[s]

    def wait(self, ekey, s, k=None):
        if k is None:
            k = self.cnt[s]
        if k is None or k <= 0:
            return
        assert k <= self.cnt[s], (ekey, s, k, self.cnt[s])
        key = (ekey, s)
        if self.waited.get(key, 0) >= k:
            return
        getattr(self.nc, ENGINE_ATTRS[ekey]).wait_ge(self.sems[s], k * self.step[s])
        self.waited[key] = k

    def eng(self, ekey):
        return getattr(self.nc, ENGINE_ATTRS[ekey])


def emit_transposes(c, nc, src_bf, dsts, ps_tr, ident_bf, dep_a=None, dep_t=None):
    last_mm = None
    for k in range(len(dsts)):
        slot = k % 2
        if c.tr_evac[slot] is not None:
            c.wait("t", "a", c.tr_evac[slot])
        if dep_a is not None:
            c.wait("t", "a", dep_a)
        last_mm = c.inc(nc.tensor.transpose(
            ps_tr[slot][:], src_bf[:, k * P:(k + 1) * P], ident_bf[:]), "t")
        c.wait("a", "t", last_mm)
        if dep_t is not None:
            c.wait("a", "t", dep_t)
        c.tr_evac[slot] = c.inc(nc.scalar.copy(dsts[k], ps_tr[slot][:]), "a")
    return last_mm, c.cnt["a"]


def build(T=4096, TP=2048, detect=False):
    assert T % TP == 0 and TP % TQW == 0
    NPASS = T // TP
    NT16 = TP // P
    TQN = TP // TQW
    NTA = T // P

    nc = bass.Bass("TRN2", target_bir_lowering=False, detect_race_conditions=detect,
                   num_devices=NCORES)

    x_in = nc.dram_tensor("x", [T, D], F32, kind="ExternalInput").ap()
    wq_d = nc.dram_tensor("wq", [D, D], F32, kind="ExternalInput").ap()
    wk_d = nc.dram_tensor("wk", [D, D], F32, kind="ExternalInput").ap()
    wv_d = nc.dram_tensor("wv", [D, D], F32, kind="ExternalInput").ap()
    wg_d = nc.dram_tensor("wg", [D, D], F32, kind="ExternalInput").ap()
    wo_d = nc.dram_tensor("wo", [D, D], F32, kind="ExternalInput").ap()
    gam_d = nc.dram_tensor("gam", [1, D], F32, kind="ExternalInput").ap()
    bet_d = nc.dram_tensor("bet", [1, D], F32, kind="ExternalInput").ap()
    wr_d = nc.dram_tensor("wr", [D, E], F32, kind="ExternalInput").ap()
    br_d = nc.dram_tensor("br", [1, E], F32, kind="ExternalInput").ap()
    w1_d = nc.dram_tensor("w1", [E, D, FF], F32, kind="ExternalInput").ap()
    b1_d = nc.dram_tensor("b1", [E, FF], F32, kind="ExternalInput").ap()
    w2_d = nc.dram_tensor("w2", [E, FF, D], F32, kind="ExternalInput").ap()
    b2_d = nc.dram_tensor("b2", [E, D], F32, kind="ExternalInput").ap()
    ident_bf_d = nc.dram_tensor("ident_bf", [P, P], BF16, kind="ExternalInput").ap()
    ident_f_d = nc.dram_tensor("ident_f", [P, P], F32, kind="ExternalInput").ap()
    ones_r_bf_d = nc.dram_tensor("ones_r_bf", [1, P], BF16, kind="ExternalInput").ap()
    ones_r_f_d = nc.dram_tensor("ones_r_f", [1, P], F32, kind="ExternalInput").ap()
    ones_c_bf_d = nc.dram_tensor("ones_c_bf", [P, 1], BF16, kind="ExternalInput").ap()
    sel_d = nc.dram_tensor("sel", [E, P], F32, kind="ExternalInput").ap()
    out_d = nc.dram_tensor("out", [T, D], F32, kind="ExternalOutput").ap()

    y1_d = nc.dram_tensor("y1_scr", [T, D], F32, kind="Internal").ap()
    y2_d = nc.dram_tensor("y2_scr", [T, D], F32, kind="Internal").ap()
    q1_sp = nc.dram_tensor("q1_sp", [T, D], F32, kind="Internal").ap()
    v_sp = nc.dram_tensor("v_sp", [T, D], F32, kind="Internal").ap()
    g_sp = nc.dram_tensor("g_sp", [T, D], F32, kind="Internal").ap()
    z_sp = nc.dram_tensor("z_sp", [T, D], BF16, kind="Internal").ap()
    cc_in = nc.dram_tensor("cc_in", [1, D], F32, kind="Internal").ap()
    cc_out = nc.dram_tensor("cc_out", [NCORES, D], F32, kind="Internal").ap()

    es = ExitStack()
    with es:
        c = Ctx(nc, es)
        for s in ("t", "v", "a", "p"):
            c.sem(s, 1)
        for s in ("lc", "wc", "ly", "la0", "la1", "lv0", "lv1", "lb0", "lb1",
                  "lgb", "sp0", "sp1", "st"):
            c.sem(s, 16)
        c.sem("cc", 1)
        for i_ in range(NW1):
            c.sem(f"w1s{i_}", 16)
        for i_ in range(NW2):
            c.sem(f"w2s{i_}", 16)

        ident_bf = es.enter_context(nc.sbuf_tensor("ident_bf_sb", [P, P], BF16))
        ident_f = es.enter_context(nc.sbuf_tensor("ident_f_sb", [P, P], F32))
        ones_r_bf = es.enter_context(nc.sbuf_tensor("ones_r_bf_sb", [1, P], BF16))
        ones_r_f = es.enter_context(nc.sbuf_tensor("ones_r_f_sb", [1, P], F32))
        ones_c_bf = es.enter_context(nc.sbuf_tensor("ones_c_bf_sb", [P, 1], BF16))
        sel_sb = es.enter_context(nc.sbuf_tensor("sel_sb", [E, P], F32))
        wr_sb = es.enter_context(nc.sbuf_tensor("wr_sb", [P, KD * E], BF16))
        br_sb = es.enter_context(nc.sbuf_tensor("br_sb", [1, E], BF16))
        b1T = es.enter_context(nc.sbuf_tensor("b1T", [P, FCH * E], F32))

        c.dma("sy", ident_bf[:], ident_bf_d, "lc")
        c.dma("sy", ident_f[:], ident_f_d, "lc")
        c.dma("sy", ones_r_bf[:], ones_r_bf_d, "lc")
        c.dma("sy", ones_r_f[:], ones_r_f_d, "lc")
        c.dma("sy", ones_c_bf[:], ones_c_bf_d, "lc")
        c.dma("sy", sel_sb[:], sel_d, "lc")
        c.dma("g", wr_sb[:].rearrange("p (k e) -> p k e", k=KD),
              wr_d.rearrange("(k p) e -> p k e", p=P), "wc")
        c.dma("g", br_sb[:], br_d, "wc")
        w_const = c.cnt["wc"]

        with nc.sbuf_tensor("b1_8", [E, FF], F32) as b1_8, \
             nc.psum_tensor("ps_b1a", [P, 16], F32) as ps_b1a, \
             nc.psum_tensor("ps_b1b", [P, 16], F32) as ps_b1b:
            c.dma("sy", b1_8[:], b1_d, "lc")
            c.wait("t", "lc")
            ps_b1 = [ps_b1a, ps_b1b]
            b1_evac = [None, None]
            for f in range(FCH):
                slot = f % 2
                if b1_evac[slot] is not None:
                    c.wait("t", "v", b1_evac[slot])
                t_mm = c.inc(nc.tensor.transpose(
                    ps_b1[slot][:, 0:E], b1_8[0:E, f * P:(f + 1) * P],
                    ident_f[0:E, 0:E]), "t")
                c.wait("v", "t", t_mm)
                b1_evac[slot] = c.inc(nc.vector.tensor_copy(
                    b1T[:, f * E:(f + 1) * E], ps_b1[slot][:, 0:E]), "v")
            c.wait("t", "v")
        c.wait("sy", "t")
        c.wait("sy", "v")

        def emit_ffn(src, dst, tag):
            with ExitStack() as fes:
                y_buf = fes.enter_context(nc.sbuf_tensor(f"yb_{tag}", [P, NT16 * D], F32))
                xnT = fes.enter_context(nc.sbuf_tensor(f"xnT_{tag}", [P, KD * TP], BF16))
                h1buf = fes.enter_context(nc.sbuf_tensor(f"h1_{tag}", [P, FGS * TP], BF16))
                w1ring = fes.enter_context(nc.sbuf_tensor(f"w1r_{tag}", [P, NW1 * KD * P], BF16))
                w2ring = fes.enter_context(nc.sbuf_tensor(f"w2r_{tag}", [P, NW2 * D], BF16))
                sq = fes.enter_context(nc.sbuf_tensor(f"sq_{tag}", [P, D], F32))
                st1 = fes.enter_context(nc.sbuf_tensor(f"st1_{tag}", [P, NT16], F32))
                st2 = fes.enter_context(nc.sbuf_tensor(f"st2_{tag}", [P, NT16], F32))
                stm = fes.enter_context(nc.sbuf_tensor(f"stm_{tag}", [P, NT16], F32))
                ste = fes.enter_context(nc.sbuf_tensor(f"ste_{tag}", [P, NT16], F32))
                stv = fes.enter_context(nc.sbuf_tensor(f"stv_{tag}", [P, NT16], F32))
                strd = fes.enter_context(nc.sbuf_tensor(f"strd_{tag}", [P, NT16], F32))
                stnm = fes.enter_context(nc.sbuf_tensor(f"stnm_{tag}", [P, NT16], F32))
                xn_sb = fes.enter_context(nc.sbuf_tensor(f"xn_{tag}", [P, D], BF16))
                lg_all = fes.enter_context(nc.sbuf_tensor(f"lg_{tag}", [P, NT16 * E], F32))
                r_m1 = fes.enter_context(nc.sbuf_tensor(f"rm1_{tag}", [P, NT16], F32))
                r_m2 = fes.enter_context(nc.sbuf_tensor(f"rm2_{tag}", [P, NT16], F32))
                r_t1 = fes.enter_context(nc.sbuf_tensor(f"rt1_{tag}", [P, NT16], F32))
                r_t2 = fes.enter_context(nc.sbuf_tensor(f"rt2_{tag}", [P, NT16], F32))
                msk1 = fes.enter_context(nc.sbuf_tensor(f"mk1_{tag}", [P, NT16 * E], F32))
                msk2 = fes.enter_context(nc.sbuf_tensor(f"mk2_{tag}", [P, NT16 * E], F32))
                l2b = fes.enter_context(nc.sbuf_tensor(f"l2_{tag}", [P, NT16 * E], F32))
                c_all = fes.enter_context(nc.sbuf_tensor(f"c_{tag}", [P, NT16 * E], F32))
                b2row = fes.enter_context(nc.sbuf_tensor(f"b2row_{tag}", [1, E * D], BF16))
                c.dma("g", b2row[:].rearrange("o (e f) -> o e f", e=E),
                      b2_d.unsqueeze(0), "wc")
                b2_wc = c.cnt["wc"]

                sched = []
                for p_i_ in range(NPASS):
                    for e in range(E):
                        for fg in range(FGN):
                            for fi in range(FGS):
                                sched.append(("w1", p_i_, e, fg * FGS + fi))
                            for fi in range(FGS):
                                sched.append(("w2", p_i_, e, fg * FGS + fi))
                w1_idx_of = {}
                w2_idx_of = {}
                i1 = i2 = 0
                for kind, p_i_, e, f in sched:
                    if kind == "w1":
                        w1_idx_of[(p_i_, e, f)] = i1
                        i1 += 1
                    else:
                        w2_idx_of[(p_i_, e, f)] = i2
                        i2 += 1
                w1_credit = {}
                w2_credit = {}
                w1_ld = {}
                w2_ld = {}
                pump_state = {"si": 0, "i1": 0, "i2": 0}

                def pump():
                    while pump_state["si"] < len(sched):
                        kind, p_i_, e, f = sched[pump_state["si"]]
                        if kind == "w1":
                            gi = pump_state["i1"]
                            if gi >= NW1:
                                cr = w1_credit.get(gi - NW1)
                                if cr is None:
                                    return
                                c.wait("g", "t", cr)
                            s_ = gi % NW1
                            c.dma("g",
                                  w1ring[:, s_ * KD * P:(s_ + 1) * KD * P]
                                  .rearrange("p (k f) -> p k f", k=KD),
                                  w1_d[e].rearrange("(k p) f -> p k f", p=P)
                                  [:, :, f * P:(f + 1) * P], f"w1s{s_}")
                            w1_ld[gi] = (f"w1s{s_}", c.cnt[f"w1s{s_}"])
                            pump_state["i1"] += 1
                        else:
                            gi = pump_state["i2"]
                            if gi >= NW2:
                                cr = w2_credit.get(gi - NW2)
                                if cr is None:
                                    return
                                c.wait("g", "t", cr)
                            s_ = gi % NW2
                            c.dma("g", w2ring[:, s_ * D:(s_ + 1) * D],
                                  w2_d[e][f * P:(f + 1) * P, :], f"w2s{s_}")
                            w2_ld[gi] = (f"w2s{s_}", c.cnt[f"w2s{s_}"])
                            pump_state["i2"] += 1
                        pump_state["si"] += 1

                for p_i in range(NPASS):
                    rows = p_i * TP
                    c.wait("t", "v")
                    c.wait("t", "a")
                    c.wait("a", "t")
                    c.wait("a", "v")
                    c.wait("sy", "st")
                    with ExitStack() as pes:
                        ps_tr = [pes.enter_context(nc.psum_tensor(
                            f"pstr{s}_{tag}{p_i}", [P, P], BF16)) for s in range(2)]
                        ps_lg = [pes.enter_context(nc.psum_tensor(
                            f"pslg{s}_{tag}{p_i}", [P, E], F32)) for s in range(2)]
                        c.tr_evac = [None, None]
                        for t in range(NT16):
                            c.dma("sy", y_buf[:, t * D:(t + 1) * D],
                                  src[rows + t * P: rows + (t + 1) * P, :], "ly")
                        ly_all = c.cnt["ly"]
                        for t in range(NT16):
                            yt = y_buf[:, t * D:(t + 1) * D]
                            c.wait("a", "ly", ly_all)
                            c.wait("a", "a")
                            c.inc(nc.scalar.activation(
                                sq[:], yt, AF.Square, accum_out=st2[:, t:t + 1]), "a")
                            c.wait("v", "ly", ly_all)
                            c.inc(nc.vector.reduce_sum(st1[:, t:t + 1], yt, AX.X), "v")
                        a_sq = c.cnt["a"]
                        v_s1 = c.cnt["v"]
                        c.wait("v", "v", v_s1)
                        c.inc(nc.vector.tensor_scalar_mul(
                            stm[:, :NT16], st1[:, :NT16], 1.0 / D), "v")
                        c.wait("v", "a", a_sq)
                        c.inc(nc.vector.tensor_scalar_mul(
                            ste[:, :NT16], st2[:, :NT16], 1.0 / D), "v")
                        c.wait("v", "v")
                        c.inc(nc.vector.tensor_tensor(
                            stv[:, :NT16], stm[:, :NT16], stm[:, :NT16], ALU.mult), "v")
                        c.wait("v", "v")
                        c.inc(nc.vector.tensor_sub(
                            stv[:, :NT16], ste[:, :NT16], stv[:, :NT16]), "v")
                        c.wait("v", "v")
                        c.inc(nc.vector.tensor_scalar_add(
                            stv[:, :NT16], stv[:, :NT16], LN_EPS), "v")
                        v_var = c.cnt["v"]
                        c.wait("a", "v", v_var)
                        c.inc(nc.scalar.activation(
                            ste[:, :NT16], stv[:, :NT16], AF.Ln), "a")
                        c.wait("v", "a")
                        c.inc(nc.vector.tensor_scalar_mul(
                            ste[:, :NT16], ste[:, :NT16], -0.5), "v")
                        c.wait("a", "v")
                        c.inc(nc.scalar.activation(
                            strd[:, :NT16], ste[:, :NT16], AF.Exp), "a")
                        a_rstd = c.cnt["a"]
                        c.wait("v", "a", a_rstd)
                        c.inc(nc.vector.tensor_tensor(
                            stnm[:, :NT16], stm[:, :NT16], strd[:, :NT16],
                            ALU.mult), "v")
                        c.wait("v", "v")
                        c.inc(nc.vector.tensor_scalar_mul(
                            stnm[:, :NT16], stnm[:, :NT16], -1.0), "v")
                        v_nm = c.cnt["v"]
                        lg_evac = [None, None]
                        last_tr = None
                        for t in range(NT16):
                            c.wait("a", "v", v_nm)
                            if last_tr is not None:
                                c.wait("a", "t", last_tr)
                            a_xn = c.inc(nc.scalar.activation(
                                xn_sb[:], y_buf[:, t * D:(t + 1) * D], AF.Identity,
                                bias=stnm[:, t:t + 1], scale=strd[:, t:t + 1]), "a")
                            dsts = [xnT[:, k * TP + t * P: k * TP + (t + 1) * P]
                                    for k in range(KD)]
                            last_tr, a_xnT = emit_transposes(
                                c, nc, xn_sb[:], dsts, ps_tr, ident_bf, dep_a=a_xn)
                            slot = t % 2
                            if lg_evac[slot] is not None:
                                c.wait("t", "v", lg_evac[slot])
                            c.wait("t", "a", a_xnT)
                            c.wait("t", "wc", w_const)
                            for k in range(KD):
                                nc.tensor.matmul(
                                    ps_lg[slot][:],
                                    xnT[:, k * TP + t * P: k * TP + (t + 1) * P],
                                    wr_sb[:, k * E:(k + 1) * E],
                                    start=(k == 0), stop=False)
                            t_lg = c.inc(nc.tensor.matmul(
                                ps_lg[slot][:], ones_r_bf[:], br_sb[:],
                                start=False, stop=True), "t")
                            last_tr = t_lg
                            c.wait("v", "t", t_lg)
                            lg_evac[slot] = c.inc(nc.vector.tensor_copy(
                                lg_all[:, t * E:(t + 1) * E], ps_lg[slot][:]), "v")
                        ne = NT16 * E
                        lg3 = lg_all[:].rearrange("p (t e) -> p t e", e=E)
                        c.wait("v", "v")
                        c.inc(nc.vector.reduce_max(r_m1[:, :NT16], lg3, AX.X), "v")
                        c.wait("v", "v")
                        c.inc(nc.vector.tensor_tensor(
                            msk1[:].rearrange("p (t e) -> p t e", e=E), lg3,
                            r_m1[:, :NT16].unsqueeze(2).broadcast_to([P, NT16, E]),
                            ALU.is_ge), "v")
                        c.wait("v", "v")
                        c.inc(nc.vector.scalar_tensor_tensor(
                            l2b[:, :ne], msk1[:, :ne], -1e30, lg_all[:, :ne],
                            ALU.mult, ALU.add), "v")
                        c.wait("v", "v")
                        c.inc(nc.vector.reduce_max(
                            r_m2[:, :NT16],
                            l2b[:].rearrange("p (t e) -> p t e", e=E), AX.X), "v")
                        c.wait("v", "v")
                        c.inc(nc.vector.tensor_tensor(
                            msk2[:].rearrange("p (t e) -> p t e", e=E),
                            l2b[:].rearrange("p (t e) -> p t e", e=E),
                            r_m2[:, :NT16].unsqueeze(2).broadcast_to([P, NT16, E]),
                            ALU.is_ge), "v")
                        c.wait("v", "v")
                        c.inc(nc.vector.tensor_sub(
                            r_t1[:, :NT16], r_m2[:, :NT16], r_m1[:, :NT16]), "v")
                        c.wait("a", "v")
                        c.inc(nc.scalar.activation(
                            r_t2[:, :NT16], r_t1[:, :NT16], AF.Exp), "a")
                        c.wait("v", "a")
                        c.inc(nc.vector.tensor_scalar_add(
                            r_t1[:, :NT16], r_t2[:, :NT16], 1.0), "v")
                        c.wait("v", "v")
                        c.inc(nc.vector.reciprocal(r_m1[:, :NT16], r_t1[:, :NT16]), "v")
                        c.wait("v", "v")
                        c.inc(nc.vector.tensor_scalar_mul(
                            r_m1[:, :NT16], r_m1[:, :NT16], 0.5), "v")
                        c.wait("v", "v")
                        c.inc(nc.vector.tensor_tensor(
                            r_m2[:, :NT16], r_t2[:, :NT16], r_m1[:, :NT16],
                            ALU.mult), "v")
                        c.wait("v", "v")
                        c.inc(nc.vector.tensor_tensor(
                            c_all[:].rearrange("p (t e) -> p t e", e=E),
                            msk1[:].rearrange("p (t e) -> p t e", e=E),
                            r_m1[:, :NT16].unsqueeze(2).broadcast_to([P, NT16, E]),
                            ALU.mult), "v")
                        c.wait("v", "v")
                        c.inc(nc.vector.tensor_tensor(
                            msk2[:].rearrange("p (t e) -> p t e", e=E),
                            msk2[:].rearrange("p (t e) -> p t e", e=E),
                            r_m2[:, :NT16].unsqueeze(2).broadcast_to([P, NT16, E]),
                            ALU.mult), "v")
                        c.wait("v", "v")
                        c.inc(nc.vector.tensor_add(
                            c_all[:, :ne], c_all[:, :ne], msk2[:, :ne]), "v")
                        v_router = c.cnt["v"]
                        a_prepass = c.cnt["a"]

                    with ExitStack() as xes:
                        ps_h = [xes.enter_context(nc.psum_tensor(
                            f"psh{s}_{tag}{p_i}", [P, TQW], F32)) for s in range(2)]
                        ps_o = [xes.enter_context(nc.psum_tensor(
                            f"pso{s}_{tag}{p_i}", [P, D], F32)) for s in range(2)]
                        pump()
                        stt_done = {}
                        h1_free = {}
                        pso_free = {}
                        psh_free = {}
                        gi2 = p_i * E * FCH
                        for e in range(E):
                            for fg in range(FGN):
                                for fi in range(FGS):
                                    f = fg * FGS + fi
                                    gi1 = w1_idx_of[(p_i, e, f)]
                                    s1_ = gi1 % NW1
                                    c.wait("t", *w1_ld[gi1])
                                    if e == 0 and fg == 0 and fi == 0:
                                        c.wait("t", "v", v_router)
                                        c.wait("t", "a", a_prepass)
                                    if h1_free.get(fi) is not None:
                                        c.wait("a", "t", h1_free[fi])
                                    for tq in range(TQN):
                                        u = (f * TQN + tq) % 2
                                        if psh_free.get(u) is not None:
                                            c.wait("t", "a", psh_free[u])
                                        for k in range(KD):
                                            ins = nc.tensor.matmul(
                                                ps_h[u][:],
                                                w1ring[:, s1_ * KD * P + k * P:
                                                       s1_ * KD * P + (k + 1) * P],
                                                xnT[:, k * TP + tq * TQW:
                                                    k * TP + (tq + 1) * TQW],
                                                start=(k == 0), stop=(k == KD - 1))
                                        t_mm = c.inc(ins, "t")
                                        if tq == TQN - 1:
                                            w1_credit[gi1] = t_mm
                                            pump()
                                        c.wait("a", "t", t_mm)
                                        psh_free[u] = c.inc(nc.scalar.activation(
                                            h1buf[:, fi * TP + tq * TQW:
                                                  fi * TP + (tq + 1) * TQW],
                                            ps_h[u][:], AF.Relu,
                                            bias=b1T[:, f * E + e: f * E + e + 1]), "a")
                                a_h1 = c.cnt["a"]
                                for g in range(NT16):
                                    so = g % 2
                                    if pso_free.get(so) is not None:
                                        c.wait("t", "v", pso_free[so])
                                    c.wait("t", "a", a_h1)
                                    if g == 0:
                                        for fi in range(FGS):
                                            c.wait("t", *w2_ld[gi2 + fi])
                                    if fg == 0:
                                        c.wait("t", "wc", b2_wc)
                                        for half in range(2):
                                            nc.tensor.matmul(
                                                ps_o[so][:, half * TQW:(half + 1) * TQW],
                                                ones_r_bf[:],
                                                b2row[0:1, e * D + half * TQW:
                                                      e * D + (half + 1) * TQW],
                                                start=True, stop=False)
                                        first = False
                                    else:
                                        first = True
                                    for fi in range(FGS):
                                        s2_ = (gi2 + fi) % NW2
                                        for half in range(2):
                                            ins = nc.tensor.matmul(
                                                ps_o[so][:, half * TQW:(half + 1) * TQW],
                                                h1buf[:, fi * TP + g * P:
                                                      fi * TP + (g + 1) * P],
                                                w2ring[:, s2_ * D + half * TQW:
                                                       s2_ * D + (half + 1) * TQW],
                                                start=(first and fi == 0),
                                                stop=(fi == FGS - 1))
                                    t_p2 = c.inc(ins, "t")
                                    if g == NT16 - 1:
                                        for fi in range(FGS):
                                            w2_credit[gi2 + fi] = t_p2
                                            h1_free[fi] = t_p2
                                        pump()
                                    c.wait("v", "t", t_p2)
                                    c.wait("v", "v", v_router)
                                    pso_free[so] = c.inc(nc.vector.scalar_tensor_tensor(
                                        y_buf[:, g * D:(g + 1) * D], ps_o[so][:],
                                        c_all[:, g * E + e: g * E + e + 1],
                                        y_buf[:, g * D:(g + 1) * D],
                                        ALU.mult, ALU.add), "v")
                                    stt_done[g] = pso_free[so]
                                gi2 += FGS
                        for t in range(NT16):
                            c.wait("sy", "v", stt_done[t])
                            c.dma("sy", dst[rows + t * P: rows + (t + 1) * P, :],
                                  y_buf[:, t * D:(t + 1) * D], "st")

        def emit_attn(src, dst):
            NB = NTA // 4
            with ExitStack() as aes:
                qk_all = aes.enter_context(nc.sbuf_tensor("qk_all", [P, NTA * 16], F32))
                ksb = aes.enter_context(nc.sbuf_tensor("ksb", [P, D], F32))
                gamB = aes.enter_context(nc.sbuf_tensor("gamB", [P, D], F32))
                betB = aes.enter_context(nc.sbuf_tensor("betB", [P, D], F32))
                gam_row = aes.enter_context(nc.sbuf_tensor("gam_row", [1, D], F32))
                bet_row = aes.enter_context(nc.sbuf_tensor("bet_row", [1, D], F32))
                ksum_sb = aes.enter_context(nc.sbuf_tensor("ksum_sb", [1, D], F32))
                ks8 = aes.enter_context(nc.sbuf_tensor("ks8", [E, D], F32))
                st1a = aes.enter_context(nc.sbuf_tensor("st1a", [P, NTA], F32))
                st2a = aes.enter_context(nc.sbuf_tensor("st2a", [P, NTA], F32))
                stma = aes.enter_context(nc.sbuf_tensor("stma", [P, NTA], F32))
                stea = aes.enter_context(nc.sbuf_tensor("stea", [P, NTA], F32))
                stva = aes.enter_context(nc.sbuf_tensor("stva", [P, NTA], F32))
                strda = aes.enter_context(nc.sbuf_tensor("strda", [P, NTA], F32))
                stnma = aes.enter_context(nc.sbuf_tensor("stnma", [P, NTA], F32))

                c.dma("sy", gam_row[:], gam_d, "lgb")
                c.dma("sy", bet_row[:], bet_d, "lgb")

                def ln_chain(j0, w, v_s1, a_sq):
                    sl = slice(j0, j0 + w)
                    c.wait("v", "v", v_s1)
                    c.inc(nc.vector.tensor_scalar_mul(
                        stma[:, sl], st1a[:, sl], 1.0 / D), "v")
                    c.wait("v", "a", a_sq)
                    c.inc(nc.vector.tensor_scalar_mul(
                        stea[:, sl], st2a[:, sl], 1.0 / D), "v")
                    c.wait("v", "v")
                    c.inc(nc.vector.tensor_tensor(
                        stva[:, sl], stma[:, sl], stma[:, sl], ALU.mult), "v")
                    c.wait("v", "v")
                    c.inc(nc.vector.tensor_sub(
                        stva[:, sl], stea[:, sl], stva[:, sl]), "v")
                    c.wait("v", "v")
                    c.inc(nc.vector.tensor_scalar_add(
                        stva[:, sl], stva[:, sl], LN_EPS), "v")
                    c.wait("a", "v")
                    c.inc(nc.scalar.activation(
                        stea[:, sl], stva[:, sl], AF.Ln), "a")
                    c.wait("v", "a")
                    c.inc(nc.vector.tensor_scalar_mul(
                        stea[:, sl], stea[:, sl], -0.5), "v")
                    c.wait("a", "v")
                    c.inc(nc.scalar.activation(
                        strda[:, sl], stea[:, sl], AF.Exp), "a")
                    a_rstd = c.cnt["a"]
                    c.wait("v", "a", a_rstd)
                    c.inc(nc.vector.tensor_tensor(
                        stnma[:, sl], stma[:, sl], strda[:, sl], ALU.mult), "v")
                    c.wait("v", "v")
                    c.inc(nc.vector.tensor_scalar_mul(
                        stnma[:, sl], stnma[:, sl], -1.0), "v")
                    return c.cnt["v"]

                with ExitStack() as s1es:
                    wqkvg = [s1es.enter_context(nc.sbuf_tensor(
                        f"w{n}_sb", [P, KD * D], BF16)) for n in "qkvg"]
                    ybuf = s1es.enter_context(nc.sbuf_tensor("a1y", [P, 8 * D], F32))
                    xn_a = [s1es.enter_context(nc.sbuf_tensor(
                        f"a1xn{i}", [P, D], BF16)) for i in range(2)]
                    xnT_a = [s1es.enter_context(nc.sbuf_tensor(
                        f"a1xnT{i}", [P, KD * P], BF16)) for i in range(2)]
                    tmpA = s1es.enter_context(nc.sbuf_tensor("a1tA", [P, D], F32))
                    tmpB = s1es.enter_context(nc.sbuf_tensor("a1tB", [P, D], F32))
                    q1b = [s1es.enter_context(nc.sbuf_tensor(
                        f"a1q{i}", [P, D], F32)) for i in range(2)]
                    vb = [s1es.enter_context(nc.sbuf_tensor(
                        f"a1v{i}", [P, D], F32)) for i in range(2)]
                    gbv = [s1es.enter_context(nc.sbuf_tensor(
                        f"a1g{i}", [P, D], F32)) for i in range(2)]
                    k1b = s1es.enter_context(nc.sbuf_tensor("a1k1", [P, D], F32))
                    k1bf = s1es.enter_context(nc.sbuf_tensor("a1k1b", [P, D], BF16))
                    sqa = s1es.enter_context(nc.sbuf_tensor("a1sq", [P, D], F32))
                    tmpQK = s1es.enter_context(nc.sbuf_tensor("a1qk", [P, D], F32))

                    for w_sb, w_dr in zip(wqkvg, (wq_d, wk_d, wv_d, wg_d)):
                        c.dma("g", w_sb[:].rearrange("p (k n) -> p k n", k=KD),
                              w_dr.rearrange("(k p) n -> p k n", p=P), "wc")
                    w_attn = c.cnt["wc"]

                    with ExitStack() as p1es:
                        ps_tr = [p1es.enter_context(nc.psum_tensor(
                            f"a1tr{i}", [P, P], BF16)) for i in range(2)]
                        ps_pj = [p1es.enter_context(nc.psum_tensor(
                            f"a1pj{i}", [P, D], F32)) for i in range(2)]
                        ps_ks = [p1es.enter_context(nc.psum_tensor(
                            f"a1ks{i}", [1, TQW], F32)) for i in range(2)]
                        c.tr_evac = [None, None]
                        pj_free = {}
                        xn_done = {}
                        spill_st = {}
                        last_proj = {}
                        prev_a_exp = None
                        prev_p_add = None
                        prev_qk_red = None
                        t_ks = None
                        lds = {}

                        def a1_load(b):
                            lsem = f"la{b % 2}"
                            for i_ in range(4):
                                t_ = 4 * b + i_
                                if t_ >= 8:
                                    c.wait("sy", "a", xn_done[t_ - 8])
                                c.dma("sy", ybuf[:, (t_ % 8) * D:(t_ % 8 + 1) * D],
                                      src[t_ * P:(t_ + 1) * P, :], lsem)
                            return lsem, c.cnt[lsem]

                        lds[0] = a1_load(0)
                        lds[1] = a1_load(1)
                        for b in range(NB):
                            lsem, ldc = lds[b]
                            j0 = 4 * b
                            for i_ in range(4):
                                t = j0 + i_
                                yt = ybuf[:, (t % 8) * D:(t % 8 + 1) * D]
                                c.wait("a", lsem, ldc)
                                c.wait("a", "a")
                                c.inc(nc.scalar.activation(
                                    sqa[:], yt, AF.Square,
                                    accum_out=st2a[:, t:t + 1]), "a")
                                c.wait("v", lsem, ldc)
                                c.inc(nc.vector.reduce_sum(
                                    st1a[:, t:t + 1], yt, AX.X), "v")
                            v_nm = ln_chain(j0, 4, c.cnt["v"], c.cnt["a"])
                            for i_ in range(4):
                                t = j0 + i_
                                tb = t % 2
                                c.wait("a", "v", v_nm)
                                if last_proj.get(t - 2) is not None:
                                    c.wait("a", "t", last_proj[t - 2])
                                a_xn = c.inc(nc.scalar.activation(
                                    xn_a[tb][:], ybuf[:, (t % 8) * D:(t % 8 + 1) * D],
                                    AF.Identity, bias=stnma[:, t:t + 1],
                                    scale=strda[:, t:t + 1]), "a")
                                xn_done[t] = a_xn
                                evac_cnt = {}

                                def emit_tr(j):
                                    slot = j % 2
                                    if c.tr_evac[slot] is not None:
                                        c.wait("t", "a", c.tr_evac[slot])
                                    c.wait("t", "a", a_xn)
                                    mm = c.inc(nc.tensor.transpose(
                                        ps_tr[slot][:],
                                        xn_a[tb][:, j * P:(j + 1) * P],
                                        ident_bf[:]), "t")
                                    c.wait("a", "t", mm)
                                    c.tr_evac[slot] = c.inc(nc.scalar.copy(
                                        xnT_a[tb][:, j * P:(j + 1) * P],
                                        ps_tr[slot][:]), "a")
                                    evac_cnt[j] = c.tr_evac[slot]

                                c.wait("t", "wc", w_attn)
                                emit_tr(0)
                                emit_tr(1)
                                v_kbf = None
                                for pi, w_sb in enumerate(wqkvg):
                                    so = pi % 2
                                    if pj_free.get(so) is not None:
                                        c.wait("t", "v", pj_free[so][0])
                                        c.wait("t", "a", pj_free[so][1])
                                    if pi >= 1:
                                        c.wait("t", "a", evac_cnt[KD - 1])
                                    for k in range(KD):
                                        if pi == 0:
                                            c.wait("t", "a", evac_cnt[k])
                                        for half in range(2):
                                            ins = nc.tensor.matmul(
                                                ps_pj[so][:, half * TQW:(half + 1) * TQW],
                                                xnT_a[tb][:, k * P:(k + 1) * P],
                                                w_sb[:, k * D + half * TQW:
                                                     k * D + (half + 1) * TQW],
                                                start=(k == 0), stop=(k == KD - 1))
                                        if pi == 0 and k + 2 < KD:
                                            emit_tr(k + 2)
                                    t_pj = c.inc(ins, "t")
                                    if pi == 3:
                                        last_proj[t] = t_pj
                                    if pi in (0, 1):
                                        dst_b = q1b[tb] if pi == 0 else k1b
                                        c.wait("v", "t", t_pj)
                                        if prev_a_exp is not None:
                                            c.wait("v", "a", prev_a_exp)
                                        v_min = c.inc(nc.vector.tensor_scalar_min(
                                            tmpA[:], ps_pj[so][:], 0.0), "v")
                                        c.wait("a", "v", v_min)
                                        if prev_p_add is not None:
                                            c.wait("a", "p", prev_p_add)
                                        a_exp = c.inc(nc.scalar.activation(
                                            tmpB[:], tmpA[:], AF.Exp), "a")
                                        prev_a_exp = a_exp
                                        c.wait("a", "t", t_pj)
                                        if pi == 0 and t >= 2:
                                            c.wait("a", *spill_st[t - 2])
                                        a_rel = c.inc(nc.scalar.activation(
                                            dst_b[:], ps_pj[so][:], AF.Relu), "a")
                                        c.wait("g", "a", a_rel)
                                        p_add = c.inc(nc.gpsimd.tensor_add(
                                            dst_b[:], tmpB[:], dst_b[:]), "p")
                                        prev_p_add = p_add
                                        pj_free[so] = (v_min, a_rel)
                                        if pi == 1:
                                            c.wait("v", "p", p_add)
                                            v_kbf = c.inc(nc.vector.tensor_copy(
                                                k1bf[:], k1b[:]), "v")
                                    else:
                                        dst_b = vb[tb] if pi == 2 else gbv[tb]
                                        c.wait("a", "t", t_pj)
                                        if t >= 2:
                                            c.wait("a", *spill_st[t - 2])
                                        a_cp = c.inc(nc.scalar.activation(
                                            dst_b[:], ps_pj[so][:], AF.Copy), "a")
                                        pj_free[so] = (c.cnt["v"], a_cp)
                                if prev_qk_red is not None:
                                    c.wait("g", "v", prev_qk_red)
                                p_qk = c.inc(nc.gpsimd.tensor_mul(
                                    tmpQK[:], q1b[tb][:], k1b[:]), "p")
                                c.wait("v", "p", p_qk)
                                prev_qk_red = c.inc(nc.vector.tensor_reduce(
                                    qk_all[:, t * 16:(t + 1) * 16],
                                    tmpQK[:].rearrange("p (h d) -> p h d", h=16),
                                    AX.X, ALU.add), "v")
                                c.wait("t", "v", v_kbf)
                                for half in range(2):
                                    ins = nc.tensor.matmul(
                                        ps_ks[half][:], ones_c_bf[:],
                                        k1bf[:, half * TQW:(half + 1) * TQW],
                                        start=(t == 0), stop=(t == NTA - 1))
                                if t == NTA - 1:
                                    t_ks = c.inc(ins, "t")
                                c.wait("sy", "v")
                                c.wait("sy", "a")
                                c.wait("sy", "p")
                                ssem = f"sp{tb}"
                                for buf, dr in ((q1b[tb], q1_sp), (vb[tb], v_sp),
                                                (gbv[tb], g_sp)):
                                    c.dma("sy", dr[t * P:(t + 1) * P, :], buf[:], ssem)
                                spill_st[t] = (ssem, c.cnt[ssem])
                            if b + 2 < NB:
                                lds[b + 2] = a1_load(b + 2)
                        c.wait("v", "t", t_ks)
                        c.inc(nc.vector.tensor_copy(
                            ksum_sb[0:1, 0:TQW], ps_ks[0][:]), "v")
                        c.wait("v", "v")
                        c.inc(nc.vector.tensor_copy(
                            ksum_sb[0:1, TQW:D], ps_ks[1][:]), "v")
                        c.wait("sy", "v")
                        c.dma("sy", cc_in, ksum_sb[:], "st")
                        c.wait("g", "st")
                        c.inc(nc.gpsimd.collective_compute(
                            "AllGather", ALU.bypass,
                            replica_groups=[list(range(NCORES))],
                            ins=[cc_in], outs=[cc_out]), "cc")
                        c.wait("sy", "cc")
                        c.dma("sy", ks8[:], cc_out, "lgb")
                with ExitStack() as kes:
                    ps_kb = kes.enter_context(nc.psum_tensor("ps_kb", [P, D], F32))
                    ps_gb = kes.enter_context(nc.psum_tensor("ps_gb", [P, D], F32))
                    c.wait("t", "v")
                    c.wait("t", "a")
                    c.wait("t", "lgb", 3)
                    for half in range(2):
                        ins = nc.tensor.matmul(
                            ps_kb[:, half * TQW:(half + 1) * TQW], sel_sb[:],
                            ks8[0:E, half * TQW:(half + 1) * TQW],
                            start=True, stop=True)
                    t_kb = c.inc(ins, "t")
                    c.wait("v", "t", t_kb)
                    c.inc(nc.vector.tensor_copy(ksb[:], ps_kb[:]), "v")
                    for half in range(2):
                        ins = nc.tensor.matmul(
                            ps_gb[:, half * TQW:(half + 1) * TQW], ones_r_f[:],
                            gam_row[0:1, half * TQW:(half + 1) * TQW],
                            start=True, stop=True)
                    t_g = c.inc(ins, "t")
                    c.wait("v", "t", t_g)
                    v_gb = c.inc(nc.vector.tensor_copy(gamB[:], ps_gb[:]), "v")
                    c.wait("t", "v", v_gb)
                    for half in range(2):
                        ins = nc.tensor.matmul(
                            ps_gb[:, half * TQW:(half + 1) * TQW], ones_r_f[:],
                            bet_row[0:1, half * TQW:(half + 1) * TQW],
                            start=True, stop=True)
                    t_b = c.inc(ins, "t")
                    c.wait("v", "t", t_b)
                    c.inc(nc.vector.tensor_copy(betB[:], ps_gb[:]), "v")
                    v_kgb = c.cnt["v"]

                with ExitStack() as s2es:
                    wo_sb = s2es.enter_context(nc.sbuf_tensor("wo_sb", [P, KD * D], BF16))
                    q1l = s2es.enter_context(nc.sbuf_tensor("a2q", [P, 4 * D], F32))
                    vl = s2es.enter_context(nc.sbuf_tensor("a2v", [P, 4 * D], F32))
                    gl = s2es.enter_context(nc.sbuf_tensor("a2g", [P, 4 * D], F32))
                    yl = s2es.enter_context(nc.sbuf_tensor("a2yl", [P, 4 * D], F32))
                    ztb = s2es.enter_context(nc.sbuf_tensor("a2zt", [P, 8 * D], F32))
                    sqz = s2es.enter_context(nc.sbuf_tensor("a2sq", [P, D], F32))
                    sqz2 = s2es.enter_context(nc.sbuf_tensor("a2sq2", [P, D], F32))
                    nrm_all = s2es.enter_context(nc.sbuf_tensor("a2nr", [P, NTA * 16], F32))
                    rq_all = s2es.enter_context(nc.sbuf_tensor("a2rq", [P, NTA * 16], F32))
                    zn = [s2es.enter_context(nc.sbuf_tensor(
                        f"a2zn{i}", [P, D], F32)) for i in range(2)]
                    sil = [s2es.enter_context(nc.sbuf_tensor(
                        f"a2sl{i}", [P, D], F32)) for i in range(2)]
                    zs = [s2es.enter_context(nc.sbuf_tensor(
                        f"a2zs{i}", [P, D], BF16)) for i in range(2)]
                    zT = [s2es.enter_context(nc.sbuf_tensor(
                        f"a2zT{i}", [P, KD * P], BF16)) for i in range(2)]
                    yo = [s2es.enter_context(nc.sbuf_tensor(
                        f"a2yo{i}", [P, D], F32)) for i in range(2)]
                    c.wait("g", "v")
                    c.wait("g", "a")
                    c.wait("sy", "t")
                    c.wait("sy", "v")
                    c.wait("sy", "a")
                    c.wait("sy", "p")
                    c.wait("sy", "st")
                    c.wait("sy", "sp0")
                    c.wait("sy", "sp1")
                    c.dma("g", wo_sb[:].rearrange("p (k n) -> p k n", k=KD),
                          wo_d.rearrange("(k p) n -> p k n", p=P), "wc")
                    w_wo = c.cnt["wc"]
                    with ExitStack() as p3es:
                        ps_tr = [p3es.enter_context(nc.psum_tensor(
                            f"a3tr{i}", [P, P], BF16)) for i in range(2)]
                        ps_ao = [p3es.enter_context(nc.psum_tensor(
                            f"a3o{i}", [P, D], F32)) for i in range(2)]
                        c.wait("t", "v")
                        c.wait("t", "a")
                        c.tr_evac = [None, None]
                        use_qv = {}
                        sil_done = {}
                        yo_done = {}
                        ao_free = {}
                        ao_mm = {}
                        zs_read = {}
                        y2st = {}
                        v_zs = {}
                        a_zn_by_t = {}
                        ld1 = {}
                        ld2 = {}
                        p_bet = {}
                        prev_nrm = None
                        pending_yo = []

                        def a2_ld1(b):
                            lsem = f"lv{b % 2}"
                            for i_ in range(4):
                                t_ = 4 * b + i_
                                if t_ >= 4:
                                    c.wait("sy", *use_qv[t_ - 4])
                                c.dma("sy", q1l[:, (t_ % 4) * D:(t_ % 4 + 1) * D],
                                      q1_sp[t_ * P:(t_ + 1) * P, :], lsem)
                                c.dma("sy", vl[:, (t_ % 4) * D:(t_ % 4 + 1) * D],
                                      v_sp[t_ * P:(t_ + 1) * P, :], lsem)
                                ld1[t_] = (lsem, c.cnt[lsem])

                        def a2_ld2(t_):
                            lsem = f"lb{t_ % 2}"
                            if t_ >= 4:
                                c.wait("sy", "a", sil_done[t_ - 4])
                                c.wait("sy", "v", yo_done[t_ - 4])
                            c.dma("sy", gl[:, (t_ % 4) * D:(t_ % 4 + 1) * D],
                                  g_sp[t_ * P:(t_ + 1) * P, :], lsem)
                            c.dma("sy", yl[:, (t_ % 4) * D:(t_ % 4 + 1) * D],
                                  src[t_ * P:(t_ + 1) * P, :], lsem)
                            ld2[t_] = (lsem, c.cnt[lsem])

                        def flush_yo(keep=0):
                            while len(pending_yo) > keep:
                                t_, t_ao_ = pending_yo.pop(0)
                                tb_ = t_ % 2
                                c.wait("v", "t", t_ao_)
                                c.wait("v", *ld2[t_])
                                if t_ >= 2:
                                    c.wait("v", *y2st[t_ - 2])
                                v_out = c.inc(nc.vector.tensor_add(
                                    yo[tb_][:], ps_ao[tb_][:],
                                    yl[:, (t_ % 4) * D:(t_ % 4 + 1) * D]), "v")
                                yo_done[t_] = v_out
                                ao_free[tb_] = v_out
                                c.wait("sy", "v", v_out)
                                c.dma("sy", dst[t_ * P:(t_ + 1) * P, :],
                                      yo[tb_][:], f"sp{tb_}")
                                y2st[t_] = (f"sp{tb_}", c.cnt[f"sp{tb_}"])
                                if t_ + 4 < NTA:
                                    a2_ld2(t_ + 4)

                        a2_ld1(0)
                        for t_ in range(min(4, NTA)):
                            a2_ld2(t_)
                        for b in range(NB):
                            j0 = 4 * b
                            zb = b % 2
                            for i_ in range(4):
                                t = j0 + i_
                                c.wait("g", *ld1[t])
                                c.wait("g", "v", v_kgb)
                                if prev_nrm is not None:
                                    c.wait("g", "v", prev_nrm)
                                p_sqz = c.inc(nc.gpsimd.tensor_mul(
                                    sqz[:], q1l[:, (t % 4) * D:(t % 4 + 1) * D],
                                    ksb[:]), "p")
                                c.wait("v", "p", p_sqz)
                                prev_nrm = c.inc(nc.vector.tensor_reduce(
                                    nrm_all[:, t * 16:(t + 1) * 16],
                                    sqz[:].rearrange("p (h d) -> p h d", h=16),
                                    AX.X, ALU.add), "v")
                            cs = slice(j0 * 16, (j0 + 4) * 16)
                            c.wait("v", "v")
                            c.inc(nc.vector.tensor_scalar_add(
                                nrm_all[:, cs], nrm_all[:, cs], NORM_EPS), "v")
                            c.wait("v", "v")
                            c.inc(nc.vector.reciprocal(
                                rq_all[:, cs], nrm_all[:, cs]), "v")
                            c.wait("v", "v")
                            c.inc(nc.vector.tensor_tensor(
                                rq_all[:, cs], qk_all[:, cs], rq_all[:, cs],
                                ALU.mult), "v")
                            v_rqv = c.cnt["v"]
                            if b >= 2:
                                c.wait("g", "a", a_zn_by_t[4 * (b - 2) + 3])
                            for i_ in range(4):
                                t = j0 + i_
                                zt_i = ztb[:, (zb * 4 + i_) * D:(zb * 4 + i_ + 1) * D]
                                c.wait("g", *ld1[t])
                                c.wait("g", "v", v_rqv)
                                c.inc(nc.gpsimd.tensor_tensor(
                                    zt_i.rearrange("p (h d) -> p h d", h=16),
                                    vl[:, (t % 4) * D:(t % 4 + 1) * D]
                                    .rearrange("p (h d) -> p h d", h=16),
                                    rq_all[:, t * 16:(t + 1) * 16]
                                    .unsqueeze(2).broadcast_to([P, 16, 64]),
                                    ALU.mult), "p")
                                use_qv[t] = ("p", c.cnt["p"])
                                c.wait("a", "p", c.cnt["p"])
                                c.wait("a", "a")
                                c.inc(nc.scalar.activation(
                                    sqz2[:], zt_i, AF.Square,
                                    accum_out=st2a[:, t:t + 1]), "a")
                                c.wait("v", "p", c.cnt["p"])
                                c.wait("v", "v")
                                c.inc(nc.vector.reduce_sum(
                                    st1a[:, t:t + 1], zt_i, AX.X), "v")
                            v_nmz = ln_chain(j0, 4, c.cnt["v"], c.cnt["a"])
                            if b + 1 < NB:
                                a2_ld1(b + 1)
                            for i_ in range(4):
                                t = j0 + i_
                                tb = t % 2
                                flush_yo(keep=1)
                                zt_i = ztb[:, (zb * 4 + i_) * D:(zb * 4 + i_ + 1) * D]
                                c.wait("a", "v", v_nmz)
                                if t >= 2:
                                    c.wait("a", "v", v_zs[t - 2])
                                    c.wait("a", "p", p_bet[t - 2])
                                a_zn = c.inc(nc.scalar.activation(
                                    zn[tb][:], zt_i, AF.Identity,
                                    bias=stnma[:, t:t + 1],
                                    scale=strda[:, t:t + 1]), "a")
                                a_zn_by_t[t] = a_zn
                                c.wait("a", *ld2[t])
                                a_sil = c.inc(nc.scalar.activation(
                                    sil[tb][:], gl[:, (t % 4) * D:(t % 4 + 1) * D],
                                    AF.Silu), "a")
                                sil_done[t] = a_sil
                                c.wait("g", "a", a_sil)
                                c.wait("g", "v", v_kgb)
                                c.inc(nc.gpsimd.tensor_mul(
                                    zn[tb][:], zn[tb][:], gamB[:]), "p")
                                c.wait("g", "p")
                                p_bet[t] = c.inc(nc.gpsimd.tensor_add(
                                    zn[tb][:], zn[tb][:], betB[:]), "p")
                                c.wait("v", "p", p_bet[t])
                                c.wait("v", "v")
                                if t >= 2:
                                    c.wait("v", "t", zs_read[t - 2])
                                v_zs[t] = c.inc(nc.vector.tensor_tensor(
                                    zs[tb][:], zn[tb][:], sil[tb][:],
                                    ALU.mult), "v")
                                evac2 = {}

                                def emit_tr2(j):
                                    slot = j % 2
                                    if c.tr_evac[slot] is not None:
                                        c.wait("t", "a", c.tr_evac[slot])
                                    c.wait("t", "v", v_zs[t])
                                    mm = c.inc(nc.tensor.transpose(
                                        ps_tr[slot][:],
                                        zs[tb][:, j * P:(j + 1) * P],
                                        ident_bf[:]), "t")
                                    c.wait("a", "t", mm)
                                    if ao_mm.get(t - 2) is not None:
                                        c.wait("a", "t", ao_mm[t - 2])
                                    c.tr_evac[slot] = c.inc(nc.scalar.copy(
                                        zT[tb][:, j * P:(j + 1) * P],
                                        ps_tr[slot][:]), "a")
                                    evac2[j] = c.tr_evac[slot]
                                    return mm

                                emit_tr2(0)
                                emit_tr2(1)
                                if ao_free.get(tb) is not None:
                                    c.wait("t", "v", ao_free[tb])
                                c.wait("t", "wc", w_wo)
                                last_tr = None
                                for k in range(KD):
                                    c.wait("t", "a", evac2[k])
                                    for half in range(2):
                                        ins = nc.tensor.matmul(
                                            ps_ao[tb][:, half * TQW:(half + 1) * TQW],
                                            zT[tb][:, k * P:(k + 1) * P],
                                            wo_sb[:, k * D + half * TQW:
                                                  k * D + (half + 1) * TQW],
                                            start=(k == 0), stop=(k == KD - 1))
                                    if k + 2 < KD:
                                        last_tr = emit_tr2(k + 2)
                                zs_read[t] = last_tr
                                t_ao = c.inc(ins, "t")
                                ao_mm[t] = t_ao
                                pending_yo.append((t, t_ao))
                        flush_yo()

        emit_ffn(x_in, y1_d, "f1")
        for ek in ("t", "v", "a", "sy", "g"):
            c.wait(ek, "t")
            c.wait(ek, "v")
            c.wait(ek, "a")
            c.wait(ek, "st")
            c.wait(ek, "ly")
        emit_attn(y1_d, y2_d)
        for ek in ("t", "v", "a", "sy", "g"):
            c.wait(ek, "t")
            c.wait(ek, "v")
            c.wait(ek, "a")
            c.wait(ek, "p")
            c.wait(ek, "st")
            c.wait(ek, "sp0")
            c.wait(ek, "sp1")
        emit_ffn(y2_d, out_d, "f2")
        c.wait("sy", "st")
    return nc


_NC_CACHE = {}


def _get_nc(T, TP):
    key = (T, TP)
    if key not in _NC_CACHE:
        _NC_CACHE[key] = build(T, TP)
    return _NC_CACHE[key]


def kernel(L0_attn_hidden_states, L0_attn_Wq, L0_attn_Wk, L0_attn_Wv, L0_attn_Wg,
           L0_attn_Wo, L0_attn_gamma, L0_attn_beta, L0_ffn_Wr, L0_ffn_br,
           L0_ffn_W1, L0_ffn_b1, L0_ffn_W2, L0_ffn_b2):
    x = np.ascontiguousarray(np.asarray(L0_attn_hidden_states, dtype=np.float32))
    B, S, H = x.shape
    xf = x.reshape(B * S, H)
    T = (B * S) // NCORES
    TP = 2048 if T % 2048 == 0 else T
    nc = _get_nc(T, TP)

    ident_bf = np.eye(P, dtype=ml_dtypes.bfloat16)
    ident_f = np.eye(P, dtype=np.float32)
    ones_r_bf = np.ones((1, P), dtype=ml_dtypes.bfloat16)
    ones_r_f = np.ones((1, P), dtype=np.float32)
    ones_c_bf = np.ones((P, 1), dtype=ml_dtypes.bfloat16)

    common = dict(
        wq=np.ascontiguousarray(L0_attn_Wq, dtype=np.float32),
        wk=np.ascontiguousarray(L0_attn_Wk, dtype=np.float32),
        wv=np.ascontiguousarray(L0_attn_Wv, dtype=np.float32),
        wg=np.ascontiguousarray(L0_attn_Wg, dtype=np.float32),
        wo=np.ascontiguousarray(L0_attn_Wo, dtype=np.float32),
        gam=np.ascontiguousarray(
            np.asarray(L0_attn_gamma, dtype=np.float32).reshape(1, H)),
        bet=np.ascontiguousarray(
            np.asarray(L0_attn_beta, dtype=np.float32).reshape(1, H)),
        wr=np.ascontiguousarray(L0_ffn_Wr, dtype=np.float32),
        br=np.ascontiguousarray(
            np.asarray(L0_ffn_br, dtype=np.float32).reshape(1, E)),
        w1=np.ascontiguousarray(L0_ffn_W1, dtype=np.float32),
        b1=np.ascontiguousarray(L0_ffn_b1, dtype=np.float32),
        w2=np.ascontiguousarray(L0_ffn_W2, dtype=np.float32),
        b2=np.ascontiguousarray(L0_ffn_b2, dtype=np.float32),
        ident_bf=ident_bf, ident_f=ident_f, ones_r_bf=ones_r_bf,
        ones_r_f=ones_r_f, ones_c_bf=ones_c_bf,
    )
    in_maps = []
    for cid in range(NCORES):
        sel = np.zeros((E, P), dtype=np.float32)
        b = cid // 2
        sel[2 * b, :] = 1.0
        sel[2 * b + 1, :] = 1.0
        im = dict(common)
        im["sel"] = sel
        im["x"] = np.ascontiguousarray(xf[cid * T:(cid + 1) * T])
        in_maps.append(im)

    res = run_bass_kernel_spmd(nc, in_maps, core_ids=list(range(NCORES)))
    out = np.concatenate([r["out"] for r in res.results], axis=0)
    return out.reshape(B, S, H)


# version 07
# speedup vs baseline: 1.0075x; 1.0075x over previous
import numpy as np
import ml_dtypes
from contextlib import ExitStack

import concourse.bass as bass
import concourse.mybir as mybir
from concourse.bass_utils import run_bass_kernel_spmd

F32 = mybir.dt.float32
BF16 = mybir.dt.bfloat16
AF = mybir.ActivationFunctionType
ALU = mybir.AluOpType
AX = mybir.AxisListType

P = 128
D = 1024
KD = D // P
E = 8
FF = 4096
FCH = FF // P
FGN = 4
FGS = FCH // FGN
NW1 = 10
NW2 = 10
LN_EPS = 1e-5
NORM_EPS = 1e-6
NCORES = 8
TQW = 512

ENGINE_ATTRS = {"t": "tensor", "v": "vector", "a": "scalar", "sy": "sync", "g": "gpsimd"}


class Ctx:
    def __init__(self, nc, es):
        self.nc = nc
        self.es = es
        self.sems = {}
        self.cnt = {}
        self.step = {}
        self.waited = {}
        self.tr_evac = [None, None]

    def sem(self, name, step=1):
        self.sems[name] = self.es.enter_context(self.nc.semaphore(name))
        self.cnt[name] = 0
        self.step[name] = step

    def inc(self, instr, s):
        self.cnt[s] += 1
        instr.then_inc(self.sems[s], self.step[s])
        return self.cnt[s]

    def dma(self, ekey, out, in_, s):
        ins = self.eng(ekey).dma_start(out, in_)
        ins.then_inc(self.sems[s], self.step[s])
        self.cnt[s] += 1
        return self.cnt[s]

    def wait(self, ekey, s, k=None):
        if k is None:
            k = self.cnt[s]
        if k is None or k <= 0:
            return
        assert k <= self.cnt[s], (ekey, s, k, self.cnt[s])
        key = (ekey, s)
        if self.waited.get(key, 0) >= k:
            return
        getattr(self.nc, ENGINE_ATTRS[ekey]).wait_ge(self.sems[s], k * self.step[s])
        self.waited[key] = k

    def eng(self, ekey):
        return getattr(self.nc, ENGINE_ATTRS[ekey])


def emit_transposes(c, nc, src_bf, dsts, ps_tr, ident_bf, dep_a=None, dep_t=None):
    last_mm = None
    for k in range(len(dsts)):
        slot = k % 2
        if c.tr_evac[slot] is not None:
            c.wait("t", "a", c.tr_evac[slot])
        if dep_a is not None:
            c.wait("t", "a", dep_a)
        last_mm = c.inc(nc.tensor.transpose(
            ps_tr[slot][:], src_bf[:, k * P:(k + 1) * P], ident_bf[:]), "t")
        c.wait("a", "t", last_mm)
        if dep_t is not None:
            c.wait("a", "t", dep_t)
        c.tr_evac[slot] = c.inc(nc.scalar.copy(dsts[k], ps_tr[slot][:]), "a")
    return last_mm, c.cnt["a"]


def build(T=4096, TP=2048, detect=False):
    assert T % TP == 0 and TP % TQW == 0
    NPASS = T // TP
    NT16 = TP // P
    TQN = TP // TQW
    NTA = T // P

    nc = bass.Bass("TRN2", target_bir_lowering=False, detect_race_conditions=detect,
                   num_devices=NCORES)

    x_in = nc.dram_tensor("x", [T, D], F32, kind="ExternalInput").ap()
    wq_d = nc.dram_tensor("wq", [D, D], F32, kind="ExternalInput").ap()
    wk_d = nc.dram_tensor("wk", [D, D], F32, kind="ExternalInput").ap()
    wv_d = nc.dram_tensor("wv", [D, D], F32, kind="ExternalInput").ap()
    wg_d = nc.dram_tensor("wg", [D, D], F32, kind="ExternalInput").ap()
    wo_d = nc.dram_tensor("wo", [D, D], F32, kind="ExternalInput").ap()
    gam_d = nc.dram_tensor("gam", [1, D], F32, kind="ExternalInput").ap()
    bet_d = nc.dram_tensor("bet", [1, D], F32, kind="ExternalInput").ap()
    wr_d = nc.dram_tensor("wr", [D, E], F32, kind="ExternalInput").ap()
    br_d = nc.dram_tensor("br", [1, E], F32, kind="ExternalInput").ap()
    w1_d = nc.dram_tensor("w1", [E, D, FF], F32, kind="ExternalInput").ap()
    b1_d = nc.dram_tensor("b1", [E, FF], F32, kind="ExternalInput").ap()
    w2_d = nc.dram_tensor("w2", [E, FF, D], F32, kind="ExternalInput").ap()
    b2_d = nc.dram_tensor("b2", [E, D], F32, kind="ExternalInput").ap()
    ident_bf_d = nc.dram_tensor("ident_bf", [P, P], BF16, kind="ExternalInput").ap()
    ident_f_d = nc.dram_tensor("ident_f", [P, P], F32, kind="ExternalInput").ap()
    ones_r_bf_d = nc.dram_tensor("ones_r_bf", [1, P], BF16, kind="ExternalInput").ap()
    ones_r_f_d = nc.dram_tensor("ones_r_f", [1, P], F32, kind="ExternalInput").ap()
    ones_c_bf_d = nc.dram_tensor("ones_c_bf", [P, 1], BF16, kind="ExternalInput").ap()
    sel_d = nc.dram_tensor("sel", [E, P], F32, kind="ExternalInput").ap()
    out_d = nc.dram_tensor("out", [T, D], F32, kind="ExternalOutput").ap()

    y1_d = nc.dram_tensor("y1_scr", [T, D], F32, kind="Internal").ap()
    y2_d = nc.dram_tensor("y2_scr", [T, D], F32, kind="Internal").ap()
    q1_sp = nc.dram_tensor("q1_sp", [T, D], F32, kind="Internal").ap()
    v_sp = nc.dram_tensor("v_sp", [T, D], F32, kind="Internal").ap()
    g_sp = nc.dram_tensor("g_sp", [T, D], F32, kind="Internal").ap()
    z_sp = nc.dram_tensor("z_sp", [T, D], BF16, kind="Internal").ap()
    cc_in = nc.dram_tensor("cc_in", [1, D], F32, kind="Internal").ap()
    cc_out = nc.dram_tensor("cc_out", [NCORES, D], F32, kind="Internal").ap()

    es = ExitStack()
    with es:
        c = Ctx(nc, es)
        for s in ("t", "v", "a", "p"):
            c.sem(s, 1)
        for s in ("lc", "wc", "ly", "la0", "la1", "lv0", "lv1", "lb0", "lb1",
                  "lgb", "sp0", "sp1", "st"):
            c.sem(s, 16)
        c.sem("cc", 1)
        for i_ in range(NW1):
            c.sem(f"w1s{i_}", 16)
        for i_ in range(NW2):
            c.sem(f"w2s{i_}", 16)

        ident_bf = es.enter_context(nc.sbuf_tensor("ident_bf_sb", [P, P], BF16))
        ident_f = es.enter_context(nc.sbuf_tensor("ident_f_sb", [P, P], F32))
        ones_r_bf = es.enter_context(nc.sbuf_tensor("ones_r_bf_sb", [1, P], BF16))
        ones_r_f = es.enter_context(nc.sbuf_tensor("ones_r_f_sb", [1, P], F32))
        ones_c_bf = es.enter_context(nc.sbuf_tensor("ones_c_bf_sb", [P, 1], BF16))
        sel_sb = es.enter_context(nc.sbuf_tensor("sel_sb", [E, P], F32))
        wr_sb = es.enter_context(nc.sbuf_tensor("wr_sb", [P, KD * E], BF16))
        br_sb = es.enter_context(nc.sbuf_tensor("br_sb", [1, E], BF16))
        b1T = es.enter_context(nc.sbuf_tensor("b1T", [P, FCH * E], F32))

        c.dma("sy", ident_bf[:], ident_bf_d, "lc")
        c.dma("sy", ident_f[:], ident_f_d, "lc")
        c.dma("sy", ones_r_bf[:], ones_r_bf_d, "lc")
        c.dma("sy", ones_r_f[:], ones_r_f_d, "lc")
        c.dma("sy", ones_c_bf[:], ones_c_bf_d, "lc")
        c.dma("sy", sel_sb[:], sel_d, "lc")
        c.dma("g", wr_sb[:].rearrange("p (k e) -> p k e", k=KD),
              wr_d.rearrange("(k p) e -> p k e", p=P), "wc")
        c.dma("g", br_sb[:], br_d, "wc")
        w_const = c.cnt["wc"]

        with nc.sbuf_tensor("b1_8", [E, FF], F32) as b1_8, \
             nc.psum_tensor("ps_b1a", [P, 16], F32) as ps_b1a, \
             nc.psum_tensor("ps_b1b", [P, 16], F32) as ps_b1b:
            c.dma("sy", b1_8[:], b1_d, "lc")
            c.wait("t", "lc")
            ps_b1 = [ps_b1a, ps_b1b]
            b1_evac = [None, None]
            for f in range(FCH):
                slot = f % 2
                if b1_evac[slot] is not None:
                    c.wait("t", "v", b1_evac[slot])
                t_mm = c.inc(nc.tensor.transpose(
                    ps_b1[slot][:, 0:E], b1_8[0:E, f * P:(f + 1) * P],
                    ident_f[0:E, 0:E]), "t")
                c.wait("v", "t", t_mm)
                b1_evac[slot] = c.inc(nc.vector.tensor_copy(
                    b1T[:, f * E:(f + 1) * E], ps_b1[slot][:, 0:E]), "v")
            c.wait("t", "v")
        c.wait("sy", "t")
        c.wait("sy", "v")

        def emit_ffn(src, dst, tag):
            with ExitStack() as fes:
                y_buf = fes.enter_context(nc.sbuf_tensor(f"yb_{tag}", [P, NT16 * D], F32))
                xnT = fes.enter_context(nc.sbuf_tensor(f"xnT_{tag}", [P, KD * TP], BF16))
                h1buf = fes.enter_context(nc.sbuf_tensor(f"h1_{tag}", [P, FGS * TP], BF16))
                w1ring = fes.enter_context(nc.sbuf_tensor(f"w1r_{tag}", [P, NW1 * KD * P], BF16))
                w2ring = fes.enter_context(nc.sbuf_tensor(f"w2r_{tag}", [P, NW2 * D], BF16))
                sq = fes.enter_context(nc.sbuf_tensor(f"sq_{tag}", [P, D], F32))
                st1 = fes.enter_context(nc.sbuf_tensor(f"st1_{tag}", [P, NT16], F32))
                st2 = fes.enter_context(nc.sbuf_tensor(f"st2_{tag}", [P, NT16], F32))
                stm = fes.enter_context(nc.sbuf_tensor(f"stm_{tag}", [P, NT16], F32))
                ste = fes.enter_context(nc.sbuf_tensor(f"ste_{tag}", [P, NT16], F32))
                stv = fes.enter_context(nc.sbuf_tensor(f"stv_{tag}", [P, NT16], F32))
                strd = fes.enter_context(nc.sbuf_tensor(f"strd_{tag}", [P, NT16], F32))
                stnm = fes.enter_context(nc.sbuf_tensor(f"stnm_{tag}", [P, NT16], F32))
                xn_sb = fes.enter_context(nc.sbuf_tensor(f"xn_{tag}", [P, D], BF16))
                lg_all = fes.enter_context(nc.sbuf_tensor(f"lg_{tag}", [P, NT16 * E], F32))
                r_m1 = fes.enter_context(nc.sbuf_tensor(f"rm1_{tag}", [P, NT16], F32))
                r_m2 = fes.enter_context(nc.sbuf_tensor(f"rm2_{tag}", [P, NT16], F32))
                r_t1 = fes.enter_context(nc.sbuf_tensor(f"rt1_{tag}", [P, NT16], F32))
                r_t2 = fes.enter_context(nc.sbuf_tensor(f"rt2_{tag}", [P, NT16], F32))
                msk1 = fes.enter_context(nc.sbuf_tensor(f"mk1_{tag}", [P, NT16 * E], F32))
                msk2 = fes.enter_context(nc.sbuf_tensor(f"mk2_{tag}", [P, NT16 * E], F32))
                l2b = fes.enter_context(nc.sbuf_tensor(f"l2_{tag}", [P, NT16 * E], F32))
                c_all = fes.enter_context(nc.sbuf_tensor(f"c_{tag}", [P, NT16 * E], F32))
                b2row = fes.enter_context(nc.sbuf_tensor(f"b2row_{tag}", [1, E * D], BF16))
                c.dma("g", b2row[:].rearrange("o (e f) -> o e f", e=E),
                      b2_d.unsqueeze(0), "wc")
                b2_wc = c.cnt["wc"]

                sched = []
                for p_i_ in range(NPASS):
                    for e in range(E):
                        for fg in range(FGN):
                            for fi in range(FGS):
                                sched.append(("w1", p_i_, e, fg * FGS + fi))
                            for fi in range(FGS):
                                sched.append(("w2", p_i_, e, fg * FGS + fi))
                w1_idx_of = {}
                w2_idx_of = {}
                i1 = i2 = 0
                for kind, p_i_, e, f in sched:
                    if kind == "w1":
                        w1_idx_of[(p_i_, e, f)] = i1
                        i1 += 1
                    else:
                        w2_idx_of[(p_i_, e, f)] = i2
                        i2 += 1
                w1_credit = {}
                w2_credit = {}
                w1_ld = {}
                w2_ld = {}
                pump_state = {"si": 0, "i1": 0, "i2": 0}

                def pump():
                    while pump_state["si"] < len(sched):
                        kind, p_i_, e, f = sched[pump_state["si"]]
                        if kind == "w1":
                            gi = pump_state["i1"]
                            if gi >= NW1:
                                cr = w1_credit.get(gi - NW1)
                                if cr is None:
                                    return
                                c.wait("g", "t", cr)
                            s_ = gi % NW1
                            c.dma("g",
                                  w1ring[:, s_ * KD * P:(s_ + 1) * KD * P]
                                  .rearrange("p (k f) -> p k f", k=KD),
                                  w1_d[e].rearrange("(k p) f -> p k f", p=P)
                                  [:, :, f * P:(f + 1) * P], f"w1s{s_}")
                            w1_ld[gi] = (f"w1s{s_}", c.cnt[f"w1s{s_}"])
                            pump_state["i1"] += 1
                        else:
                            gi = pump_state["i2"]
                            if gi >= NW2:
                                cr = w2_credit.get(gi - NW2)
                                if cr is None:
                                    return
                                c.wait("g", "t", cr)
                            s_ = gi % NW2
                            c.dma("g", w2ring[:, s_ * D:(s_ + 1) * D],
                                  w2_d[e][f * P:(f + 1) * P, :], f"w2s{s_}")
                            w2_ld[gi] = (f"w2s{s_}", c.cnt[f"w2s{s_}"])
                            pump_state["i2"] += 1
                        pump_state["si"] += 1

                for p_i in range(NPASS):
                    rows = p_i * TP
                    c.wait("t", "v")
                    c.wait("t", "a")
                    c.wait("a", "t")
                    c.wait("a", "v")
                    c.wait("sy", "st")
                    with ExitStack() as pes:
                        ps_tr = [pes.enter_context(nc.psum_tensor(
                            f"pstr{s}_{tag}{p_i}", [P, P], BF16)) for s in range(2)]
                        ps_lg = [pes.enter_context(nc.psum_tensor(
                            f"pslg{s}_{tag}{p_i}", [P, E], F32)) for s in range(2)]
                        c.tr_evac = [None, None]
                        for t in range(NT16):
                            c.dma("sy", y_buf[:, t * D:(t + 1) * D],
                                  src[rows + t * P: rows + (t + 1) * P, :], "ly")
                        ly_all = c.cnt["ly"]
                        for t in range(NT16):
                            yt = y_buf[:, t * D:(t + 1) * D]
                            c.wait("a", "ly", ly_all)
                            c.wait("a", "a")
                            c.inc(nc.scalar.activation(
                                sq[:], yt, AF.Square, accum_out=st2[:, t:t + 1]), "a")
                            c.wait("v", "ly", ly_all)
                            c.inc(nc.vector.reduce_sum(st1[:, t:t + 1], yt, AX.X), "v")
                        a_sq = c.cnt["a"]
                        v_s1 = c.cnt["v"]
                        c.wait("v", "v", v_s1)
                        c.inc(nc.vector.tensor_scalar_mul(
                            stm[:, :NT16], st1[:, :NT16], 1.0 / D), "v")
                        c.wait("v", "a", a_sq)
                        c.inc(nc.vector.tensor_scalar_mul(
                            ste[:, :NT16], st2[:, :NT16], 1.0 / D), "v")
                        c.wait("v", "v")
                        c.inc(nc.vector.tensor_tensor(
                            stv[:, :NT16], stm[:, :NT16], stm[:, :NT16], ALU.mult), "v")
                        c.wait("v", "v")
                        c.inc(nc.vector.tensor_sub(
                            stv[:, :NT16], ste[:, :NT16], stv[:, :NT16]), "v")
                        c.wait("v", "v")
                        c.inc(nc.vector.tensor_scalar_add(
                            stv[:, :NT16], stv[:, :NT16], LN_EPS), "v")
                        v_var = c.cnt["v"]
                        c.wait("a", "v", v_var)
                        c.inc(nc.scalar.activation(
                            ste[:, :NT16], stv[:, :NT16], AF.Ln), "a")
                        c.wait("v", "a")
                        c.inc(nc.vector.tensor_scalar_mul(
                            ste[:, :NT16], ste[:, :NT16], -0.5), "v")
                        c.wait("a", "v")
                        c.inc(nc.scalar.activation(
                            strd[:, :NT16], ste[:, :NT16], AF.Exp), "a")
                        a_rstd = c.cnt["a"]
                        c.wait("v", "a", a_rstd)
                        c.inc(nc.vector.tensor_tensor(
                            stnm[:, :NT16], stm[:, :NT16], strd[:, :NT16],
                            ALU.mult), "v")
                        c.wait("v", "v")
                        c.inc(nc.vector.tensor_scalar_mul(
                            stnm[:, :NT16], stnm[:, :NT16], -1.0), "v")
                        v_nm = c.cnt["v"]
                        lg_evac = [None, None]
                        last_tr = None
                        for t in range(NT16):
                            c.wait("a", "v", v_nm)
                            if last_tr is not None:
                                c.wait("a", "t", last_tr)
                            a_xn = c.inc(nc.scalar.activation(
                                xn_sb[:], y_buf[:, t * D:(t + 1) * D], AF.Identity,
                                bias=stnm[:, t:t + 1], scale=strd[:, t:t + 1]), "a")
                            dsts = [xnT[:, k * TP + t * P: k * TP + (t + 1) * P]
                                    for k in range(KD)]
                            last_tr, a_xnT = emit_transposes(
                                c, nc, xn_sb[:], dsts, ps_tr, ident_bf, dep_a=a_xn)
                            slot = t % 2
                            if lg_evac[slot] is not None:
                                c.wait("t", "v", lg_evac[slot])
                            c.wait("t", "a", a_xnT)
                            c.wait("t", "wc", w_const)
                            for k in range(KD):
                                nc.tensor.matmul(
                                    ps_lg[slot][:],
                                    xnT[:, k * TP + t * P: k * TP + (t + 1) * P],
                                    wr_sb[:, k * E:(k + 1) * E],
                                    start=(k == 0), stop=False)
                            t_lg = c.inc(nc.tensor.matmul(
                                ps_lg[slot][:], ones_r_bf[:], br_sb[:],
                                start=False, stop=True), "t")
                            last_tr = t_lg
                            c.wait("v", "t", t_lg)
                            lg_evac[slot] = c.inc(nc.vector.tensor_copy(
                                lg_all[:, t * E:(t + 1) * E], ps_lg[slot][:]), "v")
                        ne = NT16 * E
                        lg3 = lg_all[:].rearrange("p (t e) -> p t e", e=E)
                        c.wait("v", "v")
                        c.inc(nc.vector.reduce_max(r_m1[:, :NT16], lg3, AX.X), "v")
                        c.wait("v", "v")
                        c.inc(nc.vector.tensor_tensor(
                            msk1[:].rearrange("p (t e) -> p t e", e=E), lg3,
                            r_m1[:, :NT16].unsqueeze(2).broadcast_to([P, NT16, E]),
                            ALU.is_ge), "v")
                        c.wait("v", "v")
                        c.inc(nc.vector.scalar_tensor_tensor(
                            l2b[:, :ne], msk1[:, :ne], -1e30, lg_all[:, :ne],
                            ALU.mult, ALU.add), "v")
                        c.wait("v", "v")
                        c.inc(nc.vector.reduce_max(
                            r_m2[:, :NT16],
                            l2b[:].rearrange("p (t e) -> p t e", e=E), AX.X), "v")
                        c.wait("v", "v")
                        c.inc(nc.vector.tensor_tensor(
                            msk2[:].rearrange("p (t e) -> p t e", e=E),
                            l2b[:].rearrange("p (t e) -> p t e", e=E),
                            r_m2[:, :NT16].unsqueeze(2).broadcast_to([P, NT16, E]),
                            ALU.is_ge), "v")
                        c.wait("v", "v")
                        c.inc(nc.vector.tensor_sub(
                            r_t1[:, :NT16], r_m2[:, :NT16], r_m1[:, :NT16]), "v")
                        c.wait("a", "v")
                        c.inc(nc.scalar.activation(
                            r_t2[:, :NT16], r_t1[:, :NT16], AF.Exp), "a")
                        c.wait("v", "a")
                        c.inc(nc.vector.tensor_scalar_add(
                            r_t1[:, :NT16], r_t2[:, :NT16], 1.0), "v")
                        c.wait("v", "v")
                        c.inc(nc.vector.reciprocal(r_m1[:, :NT16], r_t1[:, :NT16]), "v")
                        c.wait("v", "v")
                        c.inc(nc.vector.tensor_scalar_mul(
                            r_m1[:, :NT16], r_m1[:, :NT16], 0.5), "v")
                        c.wait("v", "v")
                        c.inc(nc.vector.tensor_tensor(
                            r_m2[:, :NT16], r_t2[:, :NT16], r_m1[:, :NT16],
                            ALU.mult), "v")
                        c.wait("v", "v")
                        c.inc(nc.vector.tensor_tensor(
                            c_all[:].rearrange("p (t e) -> p t e", e=E),
                            msk1[:].rearrange("p (t e) -> p t e", e=E),
                            r_m1[:, :NT16].unsqueeze(2).broadcast_to([P, NT16, E]),
                            ALU.mult), "v")
                        c.wait("v", "v")
                        c.inc(nc.vector.tensor_tensor(
                            msk2[:].rearrange("p (t e) -> p t e", e=E),
                            msk2[:].rearrange("p (t e) -> p t e", e=E),
                            r_m2[:, :NT16].unsqueeze(2).broadcast_to([P, NT16, E]),
                            ALU.mult), "v")
                        c.wait("v", "v")
                        c.inc(nc.vector.tensor_add(
                            c_all[:, :ne], c_all[:, :ne], msk2[:, :ne]), "v")
                        v_router = c.cnt["v"]
                        a_prepass = c.cnt["a"]

                    with ExitStack() as xes:
                        ps_h = [xes.enter_context(nc.psum_tensor(
                            f"psh{s}_{tag}{p_i}", [P, TQW], F32)) for s in range(2)]
                        ps_o = [xes.enter_context(nc.psum_tensor(
                            f"pso{s}_{tag}{p_i}", [P, D], F32)) for s in range(2)]
                        pump()
                        stt_done = {}
                        h1_free = {}
                        pso_free = {}
                        psh_free = {}
                        gi2 = p_i * E * FCH
                        for e in range(E):
                            for fg in range(FGN):
                                for fi in range(FGS):
                                    f = fg * FGS + fi
                                    gi1 = w1_idx_of[(p_i, e, f)]
                                    s1_ = gi1 % NW1
                                    c.wait("t", *w1_ld[gi1])
                                    if e == 0 and fg == 0 and fi == 0:
                                        c.wait("t", "v", v_router)
                                        c.wait("t", "a", a_prepass)
                                    if h1_free.get(fi) is not None:
                                        c.wait("a", "t", h1_free[fi])
                                    for tq in range(TQN):
                                        u = (f * TQN + tq) % 2
                                        if psh_free.get(u) is not None:
                                            c.wait("t", "a", psh_free[u])
                                        for k in range(KD):
                                            ins = nc.tensor.matmul(
                                                ps_h[u][:],
                                                w1ring[:, s1_ * KD * P + k * P:
                                                       s1_ * KD * P + (k + 1) * P],
                                                xnT[:, k * TP + tq * TQW:
                                                    k * TP + (tq + 1) * TQW],
                                                start=(k == 0), stop=(k == KD - 1))
                                        t_mm = c.inc(ins, "t")
                                        if tq == TQN - 1:
                                            w1_credit[gi1] = t_mm
                                            pump()
                                        c.wait("a", "t", t_mm)
                                        psh_free[u] = c.inc(nc.scalar.activation(
                                            h1buf[:, fi * TP + tq * TQW:
                                                  fi * TP + (tq + 1) * TQW],
                                            ps_h[u][:], AF.Relu,
                                            bias=b1T[:, f * E + e: f * E + e + 1]), "a")
                                a_h1 = c.cnt["a"]
                                for g in range(NT16):
                                    so = g % 2
                                    if pso_free.get(so) is not None:
                                        c.wait("t", "v", pso_free[so])
                                    c.wait("t", "a", a_h1)
                                    if g == 0:
                                        for fi in range(FGS):
                                            c.wait("t", *w2_ld[gi2 + fi])
                                    if fg == 0:
                                        c.wait("t", "wc", b2_wc)
                                        for half in range(2):
                                            nc.tensor.matmul(
                                                ps_o[so][:, half * TQW:(half + 1) * TQW],
                                                ones_r_bf[:],
                                                b2row[0:1, e * D + half * TQW:
                                                      e * D + (half + 1) * TQW],
                                                start=True, stop=False)
                                        first = False
                                    else:
                                        first = True
                                    for fi in range(FGS):
                                        s2_ = (gi2 + fi) % NW2
                                        for half in range(2):
                                            ins = nc.tensor.matmul(
                                                ps_o[so][:, half * TQW:(half + 1) * TQW],
                                                h1buf[:, fi * TP + g * P:
                                                      fi * TP + (g + 1) * P],
                                                w2ring[:, s2_ * D + half * TQW:
                                                       s2_ * D + (half + 1) * TQW],
                                                start=(first and fi == 0),
                                                stop=(fi == FGS - 1))
                                    t_p2 = c.inc(ins, "t")
                                    if g == NT16 - 1:
                                        for fi in range(FGS):
                                            w2_credit[gi2 + fi] = t_p2
                                            h1_free[fi] = t_p2
                                        pump()
                                    c.wait("v", "t", t_p2)
                                    c.wait("v", "v", v_router)
                                    pso_free[so] = c.inc(nc.vector.scalar_tensor_tensor(
                                        y_buf[:, g * D:(g + 1) * D], ps_o[so][:],
                                        c_all[:, g * E + e: g * E + e + 1],
                                        y_buf[:, g * D:(g + 1) * D],
                                        ALU.mult, ALU.add), "v")
                                    stt_done[g] = pso_free[so]
                                gi2 += FGS
                        for t in range(NT16):
                            c.wait("sy", "v", stt_done[t])
                            c.dma("sy", dst[rows + t * P: rows + (t + 1) * P, :],
                                  y_buf[:, t * D:(t + 1) * D], "st")

        def emit_attn(src, dst):
            NB = NTA // 4
            with ExitStack() as aes:
                qk_all = aes.enter_context(nc.sbuf_tensor("qk_all", [P, NTA * 16], F32))
                ksb = aes.enter_context(nc.sbuf_tensor("ksb", [P, D], F32))
                gamB = aes.enter_context(nc.sbuf_tensor("gamB", [P, D], F32))
                betB = aes.enter_context(nc.sbuf_tensor("betB", [P, D], F32))
                gam_row = aes.enter_context(nc.sbuf_tensor("gam_row", [1, D], F32))
                bet_row = aes.enter_context(nc.sbuf_tensor("bet_row", [1, D], F32))
                ksum_sb = aes.enter_context(nc.sbuf_tensor("ksum_sb", [1, D], F32))
                ks8 = aes.enter_context(nc.sbuf_tensor("ks8", [E, D], F32))
                st1a = aes.enter_context(nc.sbuf_tensor("st1a", [P, NTA], F32))
                st2a = aes.enter_context(nc.sbuf_tensor("st2a", [P, NTA], F32))
                stma = aes.enter_context(nc.sbuf_tensor("stma", [P, NTA], F32))
                stea = aes.enter_context(nc.sbuf_tensor("stea", [P, NTA], F32))
                stva = aes.enter_context(nc.sbuf_tensor("stva", [P, NTA], F32))
                strda = aes.enter_context(nc.sbuf_tensor("strda", [P, NTA], F32))
                stnma = aes.enter_context(nc.sbuf_tensor("stnma", [P, NTA], F32))

                c.dma("sy", gam_row[:], gam_d, "lgb")
                c.dma("sy", bet_row[:], bet_d, "lgb")

                def ln_chain(j0, w, v_s1, a_sq):
                    sl = slice(j0, j0 + w)
                    c.wait("v", "v", v_s1)
                    c.inc(nc.vector.tensor_scalar_mul(
                        stma[:, sl], st1a[:, sl], 1.0 / D), "v")
                    c.wait("v", "a", a_sq)
                    c.inc(nc.vector.tensor_scalar_mul(
                        stea[:, sl], st2a[:, sl], 1.0 / D), "v")
                    c.wait("v", "v")
                    c.inc(nc.vector.tensor_tensor(
                        stva[:, sl], stma[:, sl], stma[:, sl], ALU.mult), "v")
                    c.wait("v", "v")
                    c.inc(nc.vector.tensor_sub(
                        stva[:, sl], stea[:, sl], stva[:, sl]), "v")
                    c.wait("v", "v")
                    c.inc(nc.vector.tensor_scalar_add(
                        stva[:, sl], stva[:, sl], LN_EPS), "v")
                    c.wait("a", "v")
                    c.inc(nc.scalar.activation(
                        stea[:, sl], stva[:, sl], AF.Ln), "a")
                    c.wait("v", "a")
                    c.inc(nc.vector.tensor_scalar_mul(
                        stea[:, sl], stea[:, sl], -0.5), "v")
                    c.wait("a", "v")
                    c.inc(nc.scalar.activation(
                        strda[:, sl], stea[:, sl], AF.Exp), "a")
                    a_rstd = c.cnt["a"]
                    c.wait("v", "a", a_rstd)
                    c.inc(nc.vector.tensor_tensor(
                        stnma[:, sl], stma[:, sl], strda[:, sl], ALU.mult), "v")
                    c.wait("v", "v")
                    c.inc(nc.vector.tensor_scalar_mul(
                        stnma[:, sl], stnma[:, sl], -1.0), "v")
                    return c.cnt["v"]

                with ExitStack() as s1es:
                    wqkvg = [s1es.enter_context(nc.sbuf_tensor(
                        f"w{n}_sb", [P, KD * D], BF16)) for n in "qkvg"]
                    ybuf = s1es.enter_context(nc.sbuf_tensor("a1y", [P, 8 * D], F32))
                    xn_a = [s1es.enter_context(nc.sbuf_tensor(
                        f"a1xn{i}", [P, D], BF16)) for i in range(2)]
                    xnT_a = [s1es.enter_context(nc.sbuf_tensor(
                        f"a1xnT{i}", [P, KD * P], BF16)) for i in range(2)]
                    tmpA = s1es.enter_context(nc.sbuf_tensor("a1tA", [P, D], F32))
                    tmpB = s1es.enter_context(nc.sbuf_tensor("a1tB", [P, D], F32))
                    q1b = [s1es.enter_context(nc.sbuf_tensor(
                        f"a1q{i}", [P, D], F32)) for i in range(2)]
                    vb = [s1es.enter_context(nc.sbuf_tensor(
                        f"a1v{i}", [P, D], F32)) for i in range(2)]
                    gbv = [s1es.enter_context(nc.sbuf_tensor(
                        f"a1g{i}", [P, D], F32)) for i in range(2)]
                    k1b = s1es.enter_context(nc.sbuf_tensor("a1k1", [P, D], F32))
                    k1bf = s1es.enter_context(nc.sbuf_tensor("a1k1b", [P, D], BF16))
                    sqa = s1es.enter_context(nc.sbuf_tensor("a1sq", [P, D], F32))
                    tmpQK = s1es.enter_context(nc.sbuf_tensor("a1qk", [P, D], F32))

                    for w_sb, w_dr in zip(wqkvg, (wq_d, wk_d, wv_d, wg_d)):
                        c.dma("g", w_sb[:].rearrange("p (k n) -> p k n", k=KD),
                              w_dr.rearrange("(k p) n -> p k n", p=P), "wc")
                    w_attn = c.cnt["wc"]

                    with ExitStack() as p1es:
                        ps_tr = [p1es.enter_context(nc.psum_tensor(
                            f"a1tr{i}", [P, P], BF16)) for i in range(2)]
                        ps_pj = [p1es.enter_context(nc.psum_tensor(
                            f"a1pj{i}", [P, D], F32)) for i in range(2)]
                        ps_ks = [p1es.enter_context(nc.psum_tensor(
                            f"a1ks{i}", [1, TQW], F32)) for i in range(2)]
                        c.tr_evac = [None, None]
                        pj_free = {}
                        xn_done = {}
                        spill_st = {}
                        last_proj = {}
                        prev_a_exp = None
                        prev_p_add = None
                        prev_qk_red = None
                        t_ks = None
                        lds = {}

                        def a1_load(b):
                            lsem = f"la{b % 2}"
                            for i_ in range(4):
                                t_ = 4 * b + i_
                                if t_ >= 8:
                                    c.wait("sy", "a", xn_done[t_ - 8])
                                c.dma("sy", ybuf[:, (t_ % 8) * D:(t_ % 8 + 1) * D],
                                      src[t_ * P:(t_ + 1) * P, :], lsem)
                            return lsem, c.cnt[lsem]

                        lds[0] = a1_load(0)
                        lds[1] = a1_load(1)
                        for b in range(NB):
                            lsem, ldc = lds[b]
                            j0 = 4 * b
                            for i_ in range(4):
                                t = j0 + i_
                                yt = ybuf[:, (t % 8) * D:(t % 8 + 1) * D]
                                c.wait("a", lsem, ldc)
                                c.wait("a", "a")
                                c.inc(nc.scalar.activation(
                                    sqa[:], yt, AF.Square,
                                    accum_out=st2a[:, t:t + 1]), "a")
                                c.wait("v", lsem, ldc)
                                c.inc(nc.vector.reduce_sum(
                                    st1a[:, t:t + 1], yt, AX.X), "v")
                            v_nm = ln_chain(j0, 4, c.cnt["v"], c.cnt["a"])
                            for i_ in range(4):
                                t = j0 + i_
                                tb = t % 2
                                c.wait("a", "v", v_nm)
                                if last_proj.get(t - 2) is not None:
                                    c.wait("a", "t", last_proj[t - 2])
                                a_xn = c.inc(nc.scalar.activation(
                                    xn_a[tb][:], ybuf[:, (t % 8) * D:(t % 8 + 1) * D],
                                    AF.Identity, bias=stnma[:, t:t + 1],
                                    scale=strda[:, t:t + 1]), "a")
                                xn_done[t] = a_xn
                                evac_cnt = {}

                                def emit_tr(j):
                                    slot = j % 2
                                    if c.tr_evac[slot] is not None:
                                        c.wait("t", "a", c.tr_evac[slot])
                                    c.wait("t", "a", a_xn)
                                    mm = c.inc(nc.tensor.transpose(
                                        ps_tr[slot][:],
                                        xn_a[tb][:, j * P:(j + 1) * P],
                                        ident_bf[:]), "t")
                                    c.wait("a", "t", mm)
                                    c.tr_evac[slot] = c.inc(nc.scalar.copy(
                                        xnT_a[tb][:, j * P:(j + 1) * P],
                                        ps_tr[slot][:]), "a")
                                    evac_cnt[j] = c.tr_evac[slot]

                                c.wait("t", "wc", w_attn)
                                emit_tr(0)
                                emit_tr(1)
                                v_kbf = None
                                for pi, w_sb in enumerate(wqkvg):
                                    so = pi % 2
                                    if pj_free.get(so) is not None:
                                        c.wait("t", "v", pj_free[so][0])
                                        c.wait("t", "a", pj_free[so][1])
                                    if pi >= 1:
                                        c.wait("t", "a", evac_cnt[KD - 1])
                                    for k in range(KD):
                                        if pi == 0:
                                            c.wait("t", "a", evac_cnt[k])
                                        for half in range(2):
                                            ins = nc.tensor.matmul(
                                                ps_pj[so][:, half * TQW:(half + 1) * TQW],
                                                xnT_a[tb][:, k * P:(k + 1) * P],
                                                w_sb[:, k * D + half * TQW:
                                                     k * D + (half + 1) * TQW],
                                                start=(k == 0), stop=(k == KD - 1))
                                        if pi == 0 and k + 2 < KD:
                                            emit_tr(k + 2)
                                    t_pj = c.inc(ins, "t")
                                    if pi == 3:
                                        last_proj[t] = t_pj
                                    if pi in (0, 1):
                                        dst_b = q1b[tb] if pi == 0 else k1b
                                        c.wait("v", "t", t_pj)
                                        if prev_a_exp is not None:
                                            c.wait("v", "a", prev_a_exp)
                                        v_min = c.inc(nc.vector.tensor_scalar_min(
                                            tmpA[:], ps_pj[so][:], 0.0), "v")
                                        c.wait("a", "v", v_min)
                                        if prev_p_add is not None:
                                            c.wait("a", "p", prev_p_add)
                                        a_exp = c.inc(nc.scalar.activation(
                                            tmpB[:], tmpA[:], AF.Exp), "a")
                                        prev_a_exp = a_exp
                                        c.wait("a", "t", t_pj)
                                        if pi == 0 and t >= 2:
                                            c.wait("a", *spill_st[t - 2])
                                        a_rel = c.inc(nc.scalar.activation(
                                            dst_b[:], ps_pj[so][:], AF.Relu), "a")
                                        c.wait("g", "a", a_rel)
                                        p_add = c.inc(nc.gpsimd.tensor_add(
                                            dst_b[:], tmpB[:], dst_b[:]), "p")
                                        prev_p_add = p_add
                                        pj_free[so] = (v_min, a_rel)
                                        if pi == 1:
                                            c.wait("v", "p", p_add)
                                            v_kbf = c.inc(nc.vector.tensor_copy(
                                                k1bf[:], k1b[:]), "v")
                                    else:
                                        dst_b = vb[tb] if pi == 2 else gbv[tb]
                                        c.wait("a", "t", t_pj)
                                        if t >= 2:
                                            c.wait("a", *spill_st[t - 2])
                                        a_cp = c.inc(nc.scalar.activation(
                                            dst_b[:], ps_pj[so][:], AF.Copy), "a")
                                        pj_free[so] = (c.cnt["v"], a_cp)
                                if prev_qk_red is not None:
                                    c.wait("g", "v", prev_qk_red)
                                p_qk = c.inc(nc.gpsimd.tensor_mul(
                                    tmpQK[:], q1b[tb][:], k1b[:]), "p")
                                c.wait("v", "p", p_qk)
                                prev_qk_red = c.inc(nc.vector.tensor_reduce(
                                    qk_all[:, t * 16:(t + 1) * 16],
                                    tmpQK[:].rearrange("p (h d) -> p h d", h=16),
                                    AX.X, ALU.add), "v")
                                c.wait("t", "v", v_kbf)
                                for half in range(2):
                                    ins = nc.tensor.matmul(
                                        ps_ks[half][:], ones_c_bf[:],
                                        k1bf[:, half * TQW:(half + 1) * TQW],
                                        start=(t == 0), stop=(t == NTA - 1))
                                if t == NTA - 1:
                                    t_ks = c.inc(ins, "t")
                                c.wait("sy", "v")
                                c.wait("sy", "a")
                                c.wait("sy", "p")
                                ssem = f"sp{tb}"
                                for buf, dr in ((q1b[tb], q1_sp), (vb[tb], v_sp),
                                                (gbv[tb], g_sp)):
                                    c.dma("sy", dr[t * P:(t + 1) * P, :], buf[:], ssem)
                                spill_st[t] = (ssem, c.cnt[ssem])
                            if b + 2 < NB:
                                lds[b + 2] = a1_load(b + 2)
                        c.wait("v", "t", t_ks)
                        c.inc(nc.vector.tensor_copy(
                            ksum_sb[0:1, 0:TQW], ps_ks[0][:]), "v")
                        c.wait("v", "v")
                        c.inc(nc.vector.tensor_copy(
                            ksum_sb[0:1, TQW:D], ps_ks[1][:]), "v")
                        c.wait("sy", "v")
                        c.dma("sy", cc_in, ksum_sb[:], "st")
                        c.wait("g", "st")
                        c.inc(nc.gpsimd.collective_compute(
                            "AllGather", ALU.bypass,
                            replica_groups=[list(range(NCORES))],
                            ins=[cc_in], outs=[cc_out]), "cc")
                        c.wait("sy", "cc")
                        c.dma("sy", ks8[:], cc_out, "lgb")
                with ExitStack() as kes:
                    ps_kb = kes.enter_context(nc.psum_tensor("ps_kb", [P, D], F32))
                    ps_gb = kes.enter_context(nc.psum_tensor("ps_gb", [P, D], F32))
                    c.wait("t", "v")
                    c.wait("t", "a")
                    c.wait("t", "lgb", 3)
                    for half in range(2):
                        ins = nc.tensor.matmul(
                            ps_kb[:, half * TQW:(half + 1) * TQW], sel_sb[:],
                            ks8[0:E, half * TQW:(half + 1) * TQW],
                            start=True, stop=True)
                    t_kb = c.inc(ins, "t")
                    c.wait("v", "t", t_kb)
                    c.inc(nc.vector.tensor_copy(ksb[:], ps_kb[:]), "v")
                    for half in range(2):
                        ins = nc.tensor.matmul(
                            ps_gb[:, half * TQW:(half + 1) * TQW], ones_r_f[:],
                            gam_row[0:1, half * TQW:(half + 1) * TQW],
                            start=True, stop=True)
                    t_g = c.inc(ins, "t")
                    c.wait("v", "t", t_g)
                    v_gb = c.inc(nc.vector.tensor_copy(gamB[:], ps_gb[:]), "v")
                    c.wait("t", "v", v_gb)
                    for half in range(2):
                        ins = nc.tensor.matmul(
                            ps_gb[:, half * TQW:(half + 1) * TQW], ones_r_f[:],
                            bet_row[0:1, half * TQW:(half + 1) * TQW],
                            start=True, stop=True)
                    t_b = c.inc(ins, "t")
                    c.wait("v", "t", t_b)
                    c.inc(nc.vector.tensor_copy(betB[:], ps_gb[:]), "v")
                    v_kgb = c.cnt["v"]

                with ExitStack() as s2es:
                    wo_sb = s2es.enter_context(nc.sbuf_tensor("wo_sb", [P, KD * D], BF16))
                    q1l = s2es.enter_context(nc.sbuf_tensor("a2q", [P, 4 * D], F32))
                    vl = s2es.enter_context(nc.sbuf_tensor("a2v", [P, 4 * D], F32))
                    gl = s2es.enter_context(nc.sbuf_tensor("a2g", [P, 4 * D], F32))
                    yl = s2es.enter_context(nc.sbuf_tensor("a2yl", [P, 4 * D], F32))
                    ztb = s2es.enter_context(nc.sbuf_tensor("a2zt", [P, 8 * D], F32))
                    sqz = s2es.enter_context(nc.sbuf_tensor("a2sq", [P, D], F32))
                    sqz2 = s2es.enter_context(nc.sbuf_tensor("a2sq2", [P, D], F32))
                    nrm_all = s2es.enter_context(nc.sbuf_tensor("a2nr", [P, NTA * 16], F32))
                    rq_all = s2es.enter_context(nc.sbuf_tensor("a2rq", [P, NTA * 16], F32))
                    zn = [s2es.enter_context(nc.sbuf_tensor(
                        f"a2zn{i}", [P, D], F32)) for i in range(2)]
                    sil = [s2es.enter_context(nc.sbuf_tensor(
                        f"a2sl{i}", [P, D], F32)) for i in range(2)]
                    zs = [s2es.enter_context(nc.sbuf_tensor(
                        f"a2zs{i}", [P, D], BF16)) for i in range(2)]
                    zT = [s2es.enter_context(nc.sbuf_tensor(
                        f"a2zT{i}", [P, KD * P], BF16)) for i in range(2)]
                    yo = [s2es.enter_context(nc.sbuf_tensor(
                        f"a2yo{i}", [P, D], F32)) for i in range(2)]
                    c.wait("g", "v")
                    c.wait("g", "a")
                    c.wait("sy", "t")
                    c.wait("sy", "v")
                    c.wait("sy", "a")
                    c.wait("sy", "p")
                    c.wait("sy", "st")
                    c.wait("sy", "sp0")
                    c.wait("sy", "sp1")
                    c.dma("g", wo_sb[:].rearrange("p (k n) -> p k n", k=KD),
                          wo_d.rearrange("(k p) n -> p k n", p=P), "wc")
                    w_wo = c.cnt["wc"]
                    with ExitStack() as p3es:
                        ps_tr = [p3es.enter_context(nc.psum_tensor(
                            f"a3tr{i}", [P, P], BF16)) for i in range(2)]
                        ps_ao = [p3es.enter_context(nc.psum_tensor(
                            f"a3o{i}", [P, D], F32)) for i in range(2)]
                        c.wait("t", "v")
                        c.wait("t", "a")
                        c.tr_evac = [None, None]
                        use_qv = {}
                        sil_done = {}
                        yo_done = {}
                        ao_free = {}
                        ao_mm = {}
                        zs_read = {}
                        y2st = {}
                        v_zs = {}
                        a_zn_by_t = {}
                        ld1 = {}
                        ld2 = {}
                        p_bet = {}
                        prev_nrm = None
                        pending_yo = []

                        def a2_ld1(b):
                            lsem = f"lv{b % 2}"
                            for i_ in range(4):
                                t_ = 4 * b + i_
                                if t_ >= 4:
                                    c.wait("sy", *use_qv[t_ - 4])
                                c.dma("sy", q1l[:, (t_ % 4) * D:(t_ % 4 + 1) * D],
                                      q1_sp[t_ * P:(t_ + 1) * P, :], lsem)
                                c.dma("sy", vl[:, (t_ % 4) * D:(t_ % 4 + 1) * D],
                                      v_sp[t_ * P:(t_ + 1) * P, :], lsem)
                                ld1[t_] = (lsem, c.cnt[lsem])

                        def a2_ld2(t_):
                            lsem = f"lb{t_ % 2}"
                            if t_ >= 4:
                                c.wait("sy", "a", sil_done[t_ - 4])
                                c.wait("sy", "v", yo_done[t_ - 4])
                            c.dma("sy", gl[:, (t_ % 4) * D:(t_ % 4 + 1) * D],
                                  g_sp[t_ * P:(t_ + 1) * P, :], lsem)
                            c.dma("sy", yl[:, (t_ % 4) * D:(t_ % 4 + 1) * D],
                                  src[t_ * P:(t_ + 1) * P, :], lsem)
                            ld2[t_] = (lsem, c.cnt[lsem])

                        def flush_yo(keep=0):
                            while len(pending_yo) > keep:
                                t_, t_ao_ = pending_yo.pop(0)
                                tb_ = t_ % 2
                                c.wait("v", "t", t_ao_)
                                c.wait("v", *ld2[t_])
                                if t_ >= 2:
                                    c.wait("v", *y2st[t_ - 2])
                                v_out = c.inc(nc.vector.tensor_add(
                                    yo[tb_][:], ps_ao[tb_][:],
                                    yl[:, (t_ % 4) * D:(t_ % 4 + 1) * D]), "v")
                                yo_done[t_] = v_out
                                ao_free[tb_] = v_out
                                c.wait("sy", "v", v_out)
                                c.dma("sy", dst[t_ * P:(t_ + 1) * P, :],
                                      yo[tb_][:], f"sp{tb_}")
                                y2st[t_] = (f"sp{tb_}", c.cnt[f"sp{tb_}"])
                                if t_ + 4 < NTA:
                                    a2_ld2(t_ + 4)

                        a2_ld1(0)
                        for t_ in range(min(4, NTA)):
                            a2_ld2(t_)
                        v_nmz_by_b = {}

                        def emit_stage12(b):
                            nonlocal prev_nrm
                            j0 = 4 * b
                            zb = b % 2
                            for i_ in range(4):
                                t = j0 + i_
                                c.wait("g", *ld1[t])
                                c.wait("g", "v", v_kgb)
                                if prev_nrm is not None:
                                    c.wait("g", "v", prev_nrm)
                                p_sqz = c.inc(nc.gpsimd.tensor_mul(
                                    sqz[:], q1l[:, (t % 4) * D:(t % 4 + 1) * D],
                                    ksb[:]), "p")
                                c.wait("v", "p", p_sqz)
                                prev_nrm = c.inc(nc.vector.tensor_reduce(
                                    nrm_all[:, t * 16:(t + 1) * 16],
                                    sqz[:].rearrange("p (h d) -> p h d", h=16),
                                    AX.X, ALU.add), "v")
                            cs = slice(j0 * 16, (j0 + 4) * 16)
                            c.wait("v", "v")
                            c.inc(nc.vector.tensor_scalar_add(
                                nrm_all[:, cs], nrm_all[:, cs], NORM_EPS), "v")
                            c.wait("v", "v")
                            c.inc(nc.vector.reciprocal(
                                rq_all[:, cs], nrm_all[:, cs]), "v")
                            c.wait("v", "v")
                            c.inc(nc.vector.tensor_tensor(
                                rq_all[:, cs], qk_all[:, cs], rq_all[:, cs],
                                ALU.mult), "v")
                            v_rqv = c.cnt["v"]
                            if b >= 2:
                                c.wait("g", "a", a_zn_by_t[4 * (b - 2) + 3])
                            for i_ in range(4):
                                t = j0 + i_
                                zt_i = ztb[:, (zb * 4 + i_) * D:(zb * 4 + i_ + 1) * D]
                                c.wait("g", *ld1[t])
                                c.wait("g", "v", v_rqv)
                                c.inc(nc.gpsimd.tensor_tensor(
                                    zt_i.rearrange("p (h d) -> p h d", h=16),
                                    vl[:, (t % 4) * D:(t % 4 + 1) * D]
                                    .rearrange("p (h d) -> p h d", h=16),
                                    rq_all[:, t * 16:(t + 1) * 16]
                                    .unsqueeze(2).broadcast_to([P, 16, 64]),
                                    ALU.mult), "p")
                                use_qv[t] = ("p", c.cnt["p"])
                                c.wait("a", "p", c.cnt["p"])
                                c.wait("a", "a")
                                c.inc(nc.scalar.activation(
                                    sqz2[:], zt_i, AF.Square,
                                    accum_out=st2a[:, t:t + 1]), "a")
                                c.wait("v", "p", c.cnt["p"])
                                c.wait("v", "v")
                                c.inc(nc.vector.reduce_sum(
                                    st1a[:, t:t + 1], zt_i, AX.X), "v")
                            v_nmz_by_b[b] = ln_chain(j0, 4, c.cnt["v"], c.cnt["a"])

                        def emit_stage3(b):
                            j0 = 4 * b
                            zb = b % 2
                            v_nmz = v_nmz_by_b[b]
                            for i_ in range(4):
                                t = j0 + i_
                                tb = t % 2
                                flush_yo(keep=1)
                                zt_i = ztb[:, (zb * 4 + i_) * D:(zb * 4 + i_ + 1) * D]
                                c.wait("a", "v", v_nmz)
                                if t >= 2:
                                    c.wait("a", "v", v_zs[t - 2])
                                    c.wait("a", "p", p_bet[t - 2])
                                a_zn = c.inc(nc.scalar.activation(
                                    zn[tb][:], zt_i, AF.Identity,
                                    bias=stnma[:, t:t + 1],
                                    scale=strda[:, t:t + 1]), "a")
                                a_zn_by_t[t] = a_zn
                                c.wait("a", *ld2[t])
                                a_sil = c.inc(nc.scalar.activation(
                                    sil[tb][:], gl[:, (t % 4) * D:(t % 4 + 1) * D],
                                    AF.Silu), "a")
                                sil_done[t] = a_sil
                                c.wait("g", "a", a_sil)
                                c.wait("g", "v", v_kgb)
                                c.inc(nc.gpsimd.tensor_mul(
                                    zn[tb][:], zn[tb][:], gamB[:]), "p")
                                c.wait("g", "p")
                                p_bet[t] = c.inc(nc.gpsimd.tensor_add(
                                    zn[tb][:], zn[tb][:], betB[:]), "p")
                                c.wait("v", "p", p_bet[t])
                                c.wait("v", "v")
                                if t >= 2:
                                    c.wait("v", "t", zs_read[t - 2])
                                v_zs[t] = c.inc(nc.vector.tensor_tensor(
                                    zs[tb][:], zn[tb][:], sil[tb][:],
                                    ALU.mult), "v")
                                evac2 = {}

                                def emit_tr2(j):
                                    slot = j % 2
                                    if c.tr_evac[slot] is not None:
                                        c.wait("t", "a", c.tr_evac[slot])
                                    c.wait("t", "v", v_zs[t])
                                    mm = c.inc(nc.tensor.transpose(
                                        ps_tr[slot][:],
                                        zs[tb][:, j * P:(j + 1) * P],
                                        ident_bf[:]), "t")
                                    c.wait("a", "t", mm)
                                    if ao_mm.get(t - 2) is not None:
                                        c.wait("a", "t", ao_mm[t - 2])
                                    c.tr_evac[slot] = c.inc(nc.scalar.copy(
                                        zT[tb][:, j * P:(j + 1) * P],
                                        ps_tr[slot][:]), "a")
                                    evac2[j] = c.tr_evac[slot]
                                    return mm

                                emit_tr2(0)
                                emit_tr2(1)
                                if ao_free.get(tb) is not None:
                                    c.wait("t", "v", ao_free[tb])
                                c.wait("t", "wc", w_wo)
                                last_tr = None
                                for k in range(KD):
                                    c.wait("t", "a", evac2[k])
                                    for half in range(2):
                                        ins = nc.tensor.matmul(
                                            ps_ao[tb][:, half * TQW:(half + 1) * TQW],
                                            zT[tb][:, k * P:(k + 1) * P],
                                            wo_sb[:, k * D + half * TQW:
                                                  k * D + (half + 1) * TQW],
                                            start=(k == 0), stop=(k == KD - 1))
                                    if k + 2 < KD:
                                        last_tr = emit_tr2(k + 2)
                                zs_read[t] = last_tr
                                t_ao = c.inc(ins, "t")
                                ao_mm[t] = t_ao
                                pending_yo.append((t, t_ao))

                        for b in range(NB + 1):
                            if b < NB:
                                emit_stage12(b)
                                if b + 1 < NB:
                                    a2_ld1(b + 1)
                            if b >= 1:
                                emit_stage3(b - 1)
                        flush_yo()

        emit_ffn(x_in, y1_d, "f1")
        for ek in ("t", "v", "a", "sy", "g"):
            c.wait(ek, "t")
            c.wait(ek, "v")
            c.wait(ek, "a")
            c.wait(ek, "st")
            c.wait(ek, "ly")
        emit_attn(y1_d, y2_d)
        for ek in ("t", "v", "a", "sy", "g"):
            c.wait(ek, "t")
            c.wait(ek, "v")
            c.wait(ek, "a")
            c.wait(ek, "p")
            c.wait(ek, "st")
            c.wait(ek, "sp0")
            c.wait(ek, "sp1")
        emit_ffn(y2_d, out_d, "f2")
        c.wait("sy", "st")
    return nc


_NC_CACHE = {}


def _get_nc(T, TP):
    key = (T, TP)
    if key not in _NC_CACHE:
        _NC_CACHE[key] = build(T, TP)
    return _NC_CACHE[key]


def kernel(L0_attn_hidden_states, L0_attn_Wq, L0_attn_Wk, L0_attn_Wv, L0_attn_Wg,
           L0_attn_Wo, L0_attn_gamma, L0_attn_beta, L0_ffn_Wr, L0_ffn_br,
           L0_ffn_W1, L0_ffn_b1, L0_ffn_W2, L0_ffn_b2):
    x = np.ascontiguousarray(np.asarray(L0_attn_hidden_states, dtype=np.float32))
    B, S, H = x.shape
    xf = x.reshape(B * S, H)
    T = (B * S) // NCORES
    TP = 2048 if T % 2048 == 0 else T
    nc = _get_nc(T, TP)

    ident_bf = np.eye(P, dtype=ml_dtypes.bfloat16)
    ident_f = np.eye(P, dtype=np.float32)
    ones_r_bf = np.ones((1, P), dtype=ml_dtypes.bfloat16)
    ones_r_f = np.ones((1, P), dtype=np.float32)
    ones_c_bf = np.ones((P, 1), dtype=ml_dtypes.bfloat16)

    common = dict(
        wq=np.ascontiguousarray(L0_attn_Wq, dtype=np.float32),
        wk=np.ascontiguousarray(L0_attn_Wk, dtype=np.float32),
        wv=np.ascontiguousarray(L0_attn_Wv, dtype=np.float32),
        wg=np.ascontiguousarray(L0_attn_Wg, dtype=np.float32),
        wo=np.ascontiguousarray(L0_attn_Wo, dtype=np.float32),
        gam=np.ascontiguousarray(
            np.asarray(L0_attn_gamma, dtype=np.float32).reshape(1, H)),
        bet=np.ascontiguousarray(
            np.asarray(L0_attn_beta, dtype=np.float32).reshape(1, H)),
        wr=np.ascontiguousarray(L0_ffn_Wr, dtype=np.float32),
        br=np.ascontiguousarray(
            np.asarray(L0_ffn_br, dtype=np.float32).reshape(1, E)),
        w1=np.ascontiguousarray(L0_ffn_W1, dtype=np.float32),
        b1=np.ascontiguousarray(L0_ffn_b1, dtype=np.float32),
        w2=np.ascontiguousarray(L0_ffn_W2, dtype=np.float32),
        b2=np.ascontiguousarray(L0_ffn_b2, dtype=np.float32),
        ident_bf=ident_bf, ident_f=ident_f, ones_r_bf=ones_r_bf,
        ones_r_f=ones_r_f, ones_c_bf=ones_c_bf,
    )
    in_maps = []
    for cid in range(NCORES):
        sel = np.zeros((E, P), dtype=np.float32)
        b = cid // 2
        sel[2 * b, :] = 1.0
        sel[2 * b + 1, :] = 1.0
        im = dict(common)
        im["sel"] = sel
        im["x"] = np.ascontiguousarray(xf[cid * T:(cid + 1) * T])
        in_maps.append(im)

    res = run_bass_kernel_spmd(nc, in_maps, core_ids=list(range(NCORES)))
    out = np.concatenate([r["out"] for r in res.results], axis=0)
    return out.reshape(B, S, H)


# version 08
# speedup vs baseline: 1.0080x; 1.0005x over previous
import numpy as np
import ml_dtypes
from contextlib import ExitStack

import concourse.bass as bass
import concourse.mybir as mybir
from concourse.bass_utils import run_bass_kernel_spmd

F32 = mybir.dt.float32
BF16 = mybir.dt.bfloat16
AF = mybir.ActivationFunctionType
ALU = mybir.AluOpType
AX = mybir.AxisListType

P = 128
D = 1024
KD = D // P
E = 8
FF = 4096
FCH = FF // P
FGN = 4
FGS = FCH // FGN
NW1 = 10
NW2 = 10
LN_EPS = 1e-5
NORM_EPS = 1e-6
NCORES = 8
TQW = 512

ENGINE_ATTRS = {"t": "tensor", "v": "vector", "a": "scalar", "sy": "sync", "g": "gpsimd"}


class Ctx:
    def __init__(self, nc, es):
        self.nc = nc
        self.es = es
        self.sems = {}
        self.cnt = {}
        self.step = {}
        self.waited = {}
        self.tr_evac = [None, None]

    def sem(self, name, step=1):
        self.sems[name] = self.es.enter_context(self.nc.semaphore(name))
        self.cnt[name] = 0
        self.step[name] = step

    def inc(self, instr, s):
        self.cnt[s] += 1
        instr.then_inc(self.sems[s], self.step[s])
        return self.cnt[s]

    def dma(self, ekey, out, in_, s):
        ins = self.eng(ekey).dma_start(out, in_)
        ins.then_inc(self.sems[s], self.step[s])
        self.cnt[s] += 1
        return self.cnt[s]

    def wait(self, ekey, s, k=None):
        if k is None:
            k = self.cnt[s]
        if k is None or k <= 0:
            return
        assert k <= self.cnt[s], (ekey, s, k, self.cnt[s])
        key = (ekey, s)
        if self.waited.get(key, 0) >= k:
            return
        getattr(self.nc, ENGINE_ATTRS[ekey]).wait_ge(self.sems[s], k * self.step[s])
        self.waited[key] = k

    def eng(self, ekey):
        return getattr(self.nc, ENGINE_ATTRS[ekey])


def emit_transposes(c, nc, src_bf, dsts, ps_tr, ident_bf, dep_a=None, dep_t=None):
    last_mm = None
    for k in range(len(dsts)):
        slot = k % 2
        if c.tr_evac[slot] is not None:
            c.wait("t", "a", c.tr_evac[slot])
        if dep_a is not None:
            c.wait("t", "a", dep_a)
        last_mm = c.inc(nc.tensor.transpose(
            ps_tr[slot][:], src_bf[:, k * P:(k + 1) * P], ident_bf[:]), "t")
        c.wait("a", "t", last_mm)
        if dep_t is not None:
            c.wait("a", "t", dep_t)
        c.tr_evac[slot] = c.inc(nc.scalar.copy(dsts[k], ps_tr[slot][:]), "a")
    return last_mm, c.cnt["a"]


def build(T=4096, TP=2048, detect=False):
    assert T % TP == 0 and TP % TQW == 0
    NPASS = T // TP
    NT16 = TP // P
    TQN = TP // TQW
    NTA = T // P

    nc = bass.Bass("TRN2", target_bir_lowering=False, detect_race_conditions=detect,
                   num_devices=NCORES)

    x_in = nc.dram_tensor("x", [T, D], F32, kind="ExternalInput").ap()
    wq_d = nc.dram_tensor("wq", [D, D], F32, kind="ExternalInput").ap()
    wk_d = nc.dram_tensor("wk", [D, D], F32, kind="ExternalInput").ap()
    wv_d = nc.dram_tensor("wv", [D, D], F32, kind="ExternalInput").ap()
    wg_d = nc.dram_tensor("wg", [D, D], F32, kind="ExternalInput").ap()
    wo_d = nc.dram_tensor("wo", [D, D], F32, kind="ExternalInput").ap()
    gam_d = nc.dram_tensor("gam", [1, D], F32, kind="ExternalInput").ap()
    bet_d = nc.dram_tensor("bet", [1, D], F32, kind="ExternalInput").ap()
    wr_d = nc.dram_tensor("wr", [D, E], F32, kind="ExternalInput").ap()
    br_d = nc.dram_tensor("br", [1, E], F32, kind="ExternalInput").ap()
    w1_d = nc.dram_tensor("w1", [E, D, FF], F32, kind="ExternalInput").ap()
    b1_d = nc.dram_tensor("b1", [E, FF], F32, kind="ExternalInput").ap()
    w2_d = nc.dram_tensor("w2", [E, FF, D], F32, kind="ExternalInput").ap()
    b2_d = nc.dram_tensor("b2", [E, D], F32, kind="ExternalInput").ap()
    ident_bf_d = nc.dram_tensor("ident_bf", [P, P], BF16, kind="ExternalInput").ap()
    ident_f_d = nc.dram_tensor("ident_f", [P, P], F32, kind="ExternalInput").ap()
    ones_r_bf_d = nc.dram_tensor("ones_r_bf", [1, P], BF16, kind="ExternalInput").ap()
    ones_r_f_d = nc.dram_tensor("ones_r_f", [1, P], F32, kind="ExternalInput").ap()
    ones_c_bf_d = nc.dram_tensor("ones_c_bf", [P, 1], BF16, kind="ExternalInput").ap()
    sel_d = nc.dram_tensor("sel", [E, P], F32, kind="ExternalInput").ap()
    out_d = nc.dram_tensor("out", [T, D], F32, kind="ExternalOutput").ap()

    y1_d = nc.dram_tensor("y1_scr", [T, D], F32, kind="Internal").ap()
    y2_d = nc.dram_tensor("y2_scr", [T, D], F32, kind="Internal").ap()
    q1_sp = nc.dram_tensor("q1_sp", [T, D], F32, kind="Internal").ap()
    v_sp = nc.dram_tensor("v_sp", [T, D], F32, kind="Internal").ap()
    g_sp = nc.dram_tensor("g_sp", [T, D], F32, kind="Internal").ap()
    z_sp = nc.dram_tensor("z_sp", [T, D], BF16, kind="Internal").ap()
    cc_in = nc.dram_tensor("cc_in", [1, D], F32, kind="Internal").ap()
    cc_out = nc.dram_tensor("cc_out", [NCORES, D], F32, kind="Internal").ap()

    es = ExitStack()
    with es:
        c = Ctx(nc, es)
        for s in ("t", "v", "a", "p"):
            c.sem(s, 1)
        for s in ("lc", "wc", "ly", "la0", "la1", "lv0", "lv1", "lb0", "lb1",
                  "lgb", "sp0", "sp1", "st"):
            c.sem(s, 16)
        c.sem("cc", 1)
        for i_ in range(NW1):
            c.sem(f"w1s{i_}", 16)
        for i_ in range(NW2):
            c.sem(f"w2s{i_}", 16)

        ident_bf = es.enter_context(nc.sbuf_tensor("ident_bf_sb", [P, P], BF16))
        ident_f = es.enter_context(nc.sbuf_tensor("ident_f_sb", [P, P], F32))
        ones_r_bf = es.enter_context(nc.sbuf_tensor("ones_r_bf_sb", [1, P], BF16))
        ones_r_f = es.enter_context(nc.sbuf_tensor("ones_r_f_sb", [1, P], F32))
        ones_c_bf = es.enter_context(nc.sbuf_tensor("ones_c_bf_sb", [P, 1], BF16))
        sel_sb = es.enter_context(nc.sbuf_tensor("sel_sb", [E, P], F32))
        wr_sb = es.enter_context(nc.sbuf_tensor("wr_sb", [P, KD * E], BF16))
        br_sb = es.enter_context(nc.sbuf_tensor("br_sb", [1, E], BF16))
        b1T = es.enter_context(nc.sbuf_tensor("b1T", [P, FCH * E], F32))

        c.dma("sy", ident_bf[:], ident_bf_d, "lc")
        c.dma("sy", ident_f[:], ident_f_d, "lc")
        c.dma("sy", ones_r_bf[:], ones_r_bf_d, "lc")
        c.dma("sy", ones_r_f[:], ones_r_f_d, "lc")
        c.dma("sy", ones_c_bf[:], ones_c_bf_d, "lc")
        c.dma("sy", sel_sb[:], sel_d, "lc")
        c.dma("g", wr_sb[:].rearrange("p (k e) -> p k e", k=KD),
              wr_d.rearrange("(k p) e -> p k e", p=P), "wc")
        c.dma("g", br_sb[:], br_d, "wc")
        w_const = c.cnt["wc"]

        with nc.sbuf_tensor("b1_8", [E, FF], F32) as b1_8, \
             nc.psum_tensor("ps_b1a", [P, 16], F32) as ps_b1a, \
             nc.psum_tensor("ps_b1b", [P, 16], F32) as ps_b1b:
            c.dma("sy", b1_8[:], b1_d, "lc")
            c.wait("t", "lc")
            ps_b1 = [ps_b1a, ps_b1b]
            b1_evac = [None, None]
            for f in range(FCH):
                slot = f % 2
                if b1_evac[slot] is not None:
                    c.wait("t", "v", b1_evac[slot])
                t_mm = c.inc(nc.tensor.transpose(
                    ps_b1[slot][:, 0:E], b1_8[0:E, f * P:(f + 1) * P],
                    ident_f[0:E, 0:E]), "t")
                c.wait("v", "t", t_mm)
                b1_evac[slot] = c.inc(nc.vector.tensor_copy(
                    b1T[:, f * E:(f + 1) * E], ps_b1[slot][:, 0:E]), "v")
            c.wait("t", "v")
        c.wait("sy", "t")
        c.wait("sy", "v")

        def emit_ffn(src, dst, tag):
            with ExitStack() as fes:
                y_buf = fes.enter_context(nc.sbuf_tensor(f"yb_{tag}", [P, NT16 * D], F32))
                xnT = fes.enter_context(nc.sbuf_tensor(f"xnT_{tag}", [P, KD * TP], BF16))
                h1buf = fes.enter_context(nc.sbuf_tensor(f"h1_{tag}", [P, FGS * TP], BF16))
                w1ring = fes.enter_context(nc.sbuf_tensor(f"w1r_{tag}", [P, NW1 * KD * P], BF16))
                w2ring = fes.enter_context(nc.sbuf_tensor(f"w2r_{tag}", [P, NW2 * D], BF16))
                sq = fes.enter_context(nc.sbuf_tensor(f"sq_{tag}", [P, D], F32))
                st1 = fes.enter_context(nc.sbuf_tensor(f"st1_{tag}", [P, NT16], F32))
                st2 = fes.enter_context(nc.sbuf_tensor(f"st2_{tag}", [P, NT16], F32))
                stm = fes.enter_context(nc.sbuf_tensor(f"stm_{tag}", [P, NT16], F32))
                ste = fes.enter_context(nc.sbuf_tensor(f"ste_{tag}", [P, NT16], F32))
                stv = fes.enter_context(nc.sbuf_tensor(f"stv_{tag}", [P, NT16], F32))
                strd = fes.enter_context(nc.sbuf_tensor(f"strd_{tag}", [P, NT16], F32))
                stnm = fes.enter_context(nc.sbuf_tensor(f"stnm_{tag}", [P, NT16], F32))
                xn_sb = fes.enter_context(nc.sbuf_tensor(f"xn_{tag}", [P, D], BF16))
                lg_all = fes.enter_context(nc.sbuf_tensor(f"lg_{tag}", [P, NT16 * E], F32))
                r_m1 = fes.enter_context(nc.sbuf_tensor(f"rm1_{tag}", [P, NT16], F32))
                r_m2 = fes.enter_context(nc.sbuf_tensor(f"rm2_{tag}", [P, NT16], F32))
                r_t1 = fes.enter_context(nc.sbuf_tensor(f"rt1_{tag}", [P, NT16], F32))
                r_t2 = fes.enter_context(nc.sbuf_tensor(f"rt2_{tag}", [P, NT16], F32))
                msk1 = fes.enter_context(nc.sbuf_tensor(f"mk1_{tag}", [P, NT16 * E], F32))
                msk2 = fes.enter_context(nc.sbuf_tensor(f"mk2_{tag}", [P, NT16 * E], F32))
                l2b = fes.enter_context(nc.sbuf_tensor(f"l2_{tag}", [P, NT16 * E], F32))
                c_all = fes.enter_context(nc.sbuf_tensor(f"c_{tag}", [P, NT16 * E], F32))
                b2row = fes.enter_context(nc.sbuf_tensor(f"b2row_{tag}", [1, E * D], BF16))
                c.dma("g", b2row[:].rearrange("o (e f) -> o e f", e=E),
                      b2_d.unsqueeze(0), "wc")
                b2_wc = c.cnt["wc"]

                sched = []
                for p_i_ in range(NPASS):
                    for e in range(E):
                        for fg in range(FGN):
                            for fi in range(FGS):
                                sched.append(("w1", p_i_, e, fg * FGS + fi))
                            for fi in range(FGS):
                                sched.append(("w2", p_i_, e, fg * FGS + fi))
                w1_idx_of = {}
                w2_idx_of = {}
                i1 = i2 = 0
                for kind, p_i_, e, f in sched:
                    if kind == "w1":
                        w1_idx_of[(p_i_, e, f)] = i1
                        i1 += 1
                    else:
                        w2_idx_of[(p_i_, e, f)] = i2
                        i2 += 1
                w1_credit = {}
                w2_credit = {}
                w1_ld = {}
                w2_ld = {}
                pump_state = {"si": 0, "i1": 0, "i2": 0}

                def pump():
                    while pump_state["si"] < len(sched):
                        kind, p_i_, e, f = sched[pump_state["si"]]
                        if kind == "w1":
                            gi = pump_state["i1"]
                            if gi >= NW1:
                                cr = w1_credit.get(gi - NW1)
                                if cr is None:
                                    return
                                c.wait("g", "t", cr)
                            s_ = gi % NW1
                            c.dma("g",
                                  w1ring[:, s_ * KD * P:(s_ + 1) * KD * P]
                                  .rearrange("p (k f) -> p k f", k=KD),
                                  w1_d[e].rearrange("(k p) f -> p k f", p=P)
                                  [:, :, f * P:(f + 1) * P], f"w1s{s_}")
                            w1_ld[gi] = (f"w1s{s_}", c.cnt[f"w1s{s_}"])
                            pump_state["i1"] += 1
                        else:
                            gi = pump_state["i2"]
                            if gi >= NW2:
                                cr = w2_credit.get(gi - NW2)
                                if cr is None:
                                    return
                                c.wait("g", "t", cr)
                            s_ = gi % NW2
                            c.dma("g", w2ring[:, s_ * D:(s_ + 1) * D],
                                  w2_d[e][f * P:(f + 1) * P, :], f"w2s{s_}")
                            w2_ld[gi] = (f"w2s{s_}", c.cnt[f"w2s{s_}"])
                            pump_state["i2"] += 1
                        pump_state["si"] += 1

                for p_i in range(NPASS):
                    rows = p_i * TP
                    c.wait("t", "v")
                    c.wait("t", "a")
                    c.wait("a", "t")
                    c.wait("a", "v")
                    c.wait("sy", "st")
                    with ExitStack() as pes:
                        ps_tr = [pes.enter_context(nc.psum_tensor(
                            f"pstr{s}_{tag}{p_i}", [P, P], BF16)) for s in range(2)]
                        ps_lg = [pes.enter_context(nc.psum_tensor(
                            f"pslg{s}_{tag}{p_i}", [P, E], F32)) for s in range(2)]
                        c.tr_evac = [None, None]
                        for t in range(NT16):
                            c.dma("sy", y_buf[:, t * D:(t + 1) * D],
                                  src[rows + t * P: rows + (t + 1) * P, :], "ly")
                        ly_all = c.cnt["ly"]
                        for t in range(NT16):
                            yt = y_buf[:, t * D:(t + 1) * D]
                            c.wait("a", "ly", ly_all)
                            c.wait("a", "a")
                            c.inc(nc.scalar.activation(
                                sq[:], yt, AF.Square, accum_out=st2[:, t:t + 1]), "a")
                            c.wait("v", "ly", ly_all)
                            c.inc(nc.vector.reduce_sum(st1[:, t:t + 1], yt, AX.X), "v")
                        a_sq = c.cnt["a"]
                        v_s1 = c.cnt["v"]
                        c.wait("v", "v", v_s1)
                        c.inc(nc.vector.tensor_scalar_mul(
                            stm[:, :NT16], st1[:, :NT16], 1.0 / D), "v")
                        c.wait("v", "a", a_sq)
                        c.inc(nc.vector.tensor_scalar_mul(
                            ste[:, :NT16], st2[:, :NT16], 1.0 / D), "v")
                        c.wait("v", "v")
                        c.inc(nc.vector.tensor_tensor(
                            stv[:, :NT16], stm[:, :NT16], stm[:, :NT16], ALU.mult), "v")
                        c.wait("v", "v")
                        c.inc(nc.vector.tensor_sub(
                            stv[:, :NT16], ste[:, :NT16], stv[:, :NT16]), "v")
                        c.wait("v", "v")
                        c.inc(nc.vector.tensor_scalar_add(
                            stv[:, :NT16], stv[:, :NT16], LN_EPS), "v")
                        v_var = c.cnt["v"]
                        c.wait("a", "v", v_var)
                        c.inc(nc.scalar.activation(
                            ste[:, :NT16], stv[:, :NT16], AF.Ln), "a")
                        c.wait("v", "a")
                        c.inc(nc.vector.tensor_scalar_mul(
                            ste[:, :NT16], ste[:, :NT16], -0.5), "v")
                        c.wait("a", "v")
                        c.inc(nc.scalar.activation(
                            strd[:, :NT16], ste[:, :NT16], AF.Exp), "a")
                        a_rstd = c.cnt["a"]
                        c.wait("v", "a", a_rstd)
                        c.inc(nc.vector.tensor_tensor(
                            stnm[:, :NT16], stm[:, :NT16], strd[:, :NT16],
                            ALU.mult), "v")
                        c.wait("v", "v")
                        c.inc(nc.vector.tensor_scalar_mul(
                            stnm[:, :NT16], stnm[:, :NT16], -1.0), "v")
                        v_nm = c.cnt["v"]
                        lg_evac = [None, None]
                        last_tr = None
                        for t in range(NT16):
                            c.wait("a", "v", v_nm)
                            if last_tr is not None:
                                c.wait("a", "t", last_tr)
                            a_xn = c.inc(nc.scalar.activation(
                                xn_sb[:], y_buf[:, t * D:(t + 1) * D], AF.Identity,
                                bias=stnm[:, t:t + 1], scale=strd[:, t:t + 1]), "a")
                            dsts = [xnT[:, k * TP + t * P: k * TP + (t + 1) * P]
                                    for k in range(KD)]
                            last_tr, a_xnT = emit_transposes(
                                c, nc, xn_sb[:], dsts, ps_tr, ident_bf, dep_a=a_xn)
                            slot = t % 2
                            if lg_evac[slot] is not None:
                                c.wait("t", "v", lg_evac[slot])
                            c.wait("t", "a", a_xnT)
                            c.wait("t", "wc", w_const)
                            for k in range(KD):
                                nc.tensor.matmul(
                                    ps_lg[slot][:],
                                    xnT[:, k * TP + t * P: k * TP + (t + 1) * P],
                                    wr_sb[:, k * E:(k + 1) * E],
                                    start=(k == 0), stop=False)
                            t_lg = c.inc(nc.tensor.matmul(
                                ps_lg[slot][:], ones_r_bf[:], br_sb[:],
                                start=False, stop=True), "t")
                            last_tr = t_lg
                            c.wait("v", "t", t_lg)
                            lg_evac[slot] = c.inc(nc.vector.tensor_copy(
                                lg_all[:, t * E:(t + 1) * E], ps_lg[slot][:]), "v")
                        ne = NT16 * E
                        lg3 = lg_all[:].rearrange("p (t e) -> p t e", e=E)
                        c.wait("v", "v")
                        c.inc(nc.vector.reduce_max(r_m1[:, :NT16], lg3, AX.X), "v")
                        c.wait("v", "v")
                        c.inc(nc.vector.tensor_tensor(
                            msk1[:].rearrange("p (t e) -> p t e", e=E), lg3,
                            r_m1[:, :NT16].unsqueeze(2).broadcast_to([P, NT16, E]),
                            ALU.is_ge), "v")
                        c.wait("v", "v")
                        c.inc(nc.vector.scalar_tensor_tensor(
                            l2b[:, :ne], msk1[:, :ne], -1e30, lg_all[:, :ne],
                            ALU.mult, ALU.add), "v")
                        c.wait("v", "v")
                        c.inc(nc.vector.reduce_max(
                            r_m2[:, :NT16],
                            l2b[:].rearrange("p (t e) -> p t e", e=E), AX.X), "v")
                        c.wait("v", "v")
                        c.inc(nc.vector.tensor_tensor(
                            msk2[:].rearrange("p (t e) -> p t e", e=E),
                            l2b[:].rearrange("p (t e) -> p t e", e=E),
                            r_m2[:, :NT16].unsqueeze(2).broadcast_to([P, NT16, E]),
                            ALU.is_ge), "v")
                        c.wait("v", "v")
                        c.inc(nc.vector.tensor_sub(
                            r_t1[:, :NT16], r_m2[:, :NT16], r_m1[:, :NT16]), "v")
                        c.wait("a", "v")
                        c.inc(nc.scalar.activation(
                            r_t2[:, :NT16], r_t1[:, :NT16], AF.Exp), "a")
                        c.wait("v", "a")
                        c.inc(nc.vector.tensor_scalar_add(
                            r_t1[:, :NT16], r_t2[:, :NT16], 1.0), "v")
                        c.wait("v", "v")
                        c.inc(nc.vector.reciprocal(r_m1[:, :NT16], r_t1[:, :NT16]), "v")
                        c.wait("v", "v")
                        c.inc(nc.vector.tensor_scalar_mul(
                            r_m1[:, :NT16], r_m1[:, :NT16], 0.5), "v")
                        c.wait("v", "v")
                        c.inc(nc.vector.tensor_tensor(
                            r_m2[:, :NT16], r_t2[:, :NT16], r_m1[:, :NT16],
                            ALU.mult), "v")
                        c.wait("v", "v")
                        c.inc(nc.vector.tensor_tensor(
                            c_all[:].rearrange("p (t e) -> p t e", e=E),
                            msk1[:].rearrange("p (t e) -> p t e", e=E),
                            r_m1[:, :NT16].unsqueeze(2).broadcast_to([P, NT16, E]),
                            ALU.mult), "v")
                        c.wait("v", "v")
                        c.inc(nc.vector.tensor_tensor(
                            msk2[:].rearrange("p (t e) -> p t e", e=E),
                            msk2[:].rearrange("p (t e) -> p t e", e=E),
                            r_m2[:, :NT16].unsqueeze(2).broadcast_to([P, NT16, E]),
                            ALU.mult), "v")
                        c.wait("v", "v")
                        c.inc(nc.vector.tensor_add(
                            c_all[:, :ne], c_all[:, :ne], msk2[:, :ne]), "v")
                        v_router = c.cnt["v"]
                        a_prepass = c.cnt["a"]

                    with ExitStack() as xes:
                        ps_h = [xes.enter_context(nc.psum_tensor(
                            f"psh{s}_{tag}{p_i}", [P, TQW], F32)) for s in range(2)]
                        ps_o = [xes.enter_context(nc.psum_tensor(
                            f"pso{s}_{tag}{p_i}", [P, D], F32)) for s in range(2)]
                        pump()
                        stt_done = {}
                        h1_free = {}
                        pso_free = {}
                        psh_free = {}
                        gi2 = p_i * E * FCH
                        for e in range(E):
                            for fg in range(FGN):
                                for fi in range(FGS):
                                    f = fg * FGS + fi
                                    gi1 = w1_idx_of[(p_i, e, f)]
                                    s1_ = gi1 % NW1
                                    c.wait("t", *w1_ld[gi1])
                                    if e == 0 and fg == 0 and fi == 0:
                                        c.wait("t", "v", v_router)
                                        c.wait("t", "a", a_prepass)
                                    if h1_free.get(fi) is not None:
                                        c.wait("a", "t", h1_free[fi])
                                    for tq in range(TQN):
                                        u = (f * TQN + tq) % 2
                                        if psh_free.get(u) is not None:
                                            c.wait("t", "a", psh_free[u])
                                        for k in range(KD):
                                            ins = nc.tensor.matmul(
                                                ps_h[u][:],
                                                w1ring[:, s1_ * KD * P + k * P:
                                                       s1_ * KD * P + (k + 1) * P],
                                                xnT[:, k * TP + tq * TQW:
                                                    k * TP + (tq + 1) * TQW],
                                                start=(k == 0), stop=(k == KD - 1))
                                        t_mm = c.inc(ins, "t")
                                        if tq == TQN - 1:
                                            w1_credit[gi1] = t_mm
                                            pump()
                                        c.wait("a", "t", t_mm)
                                        psh_free[u] = c.inc(nc.scalar.activation(
                                            h1buf[:, fi * TP + tq * TQW:
                                                  fi * TP + (tq + 1) * TQW],
                                            ps_h[u][:], AF.Relu,
                                            bias=b1T[:, f * E + e: f * E + e + 1]), "a")
                                a_h1 = c.cnt["a"]
                                for g in range(NT16):
                                    so = g % 2
                                    if pso_free.get(so) is not None:
                                        c.wait("t", "v", pso_free[so])
                                    c.wait("t", "a", a_h1)
                                    if g == 0:
                                        for fi in range(FGS):
                                            c.wait("t", *w2_ld[gi2 + fi])
                                    if fg == 0:
                                        c.wait("t", "wc", b2_wc)
                                        for half in range(2):
                                            nc.tensor.matmul(
                                                ps_o[so][:, half * TQW:(half + 1) * TQW],
                                                ones_r_bf[:],
                                                b2row[0:1, e * D + half * TQW:
                                                      e * D + (half + 1) * TQW],
                                                start=True, stop=False)
                                        first = False
                                    else:
                                        first = True
                                    for fi in range(FGS):
                                        s2_ = (gi2 + fi) % NW2
                                        for half in range(2):
                                            ins = nc.tensor.matmul(
                                                ps_o[so][:, half * TQW:(half + 1) * TQW],
                                                h1buf[:, fi * TP + g * P:
                                                      fi * TP + (g + 1) * P],
                                                w2ring[:, s2_ * D + half * TQW:
                                                       s2_ * D + (half + 1) * TQW],
                                                start=(first and fi == 0),
                                                stop=(fi == FGS - 1))
                                    t_p2 = c.inc(ins, "t")
                                    if g == NT16 - 1:
                                        for fi in range(FGS):
                                            w2_credit[gi2 + fi] = t_p2
                                            h1_free[fi] = t_p2
                                        pump()
                                    c.wait("v", "t", t_p2)
                                    c.wait("v", "v", v_router)
                                    pso_free[so] = c.inc(nc.vector.scalar_tensor_tensor(
                                        y_buf[:, g * D:(g + 1) * D], ps_o[so][:],
                                        c_all[:, g * E + e: g * E + e + 1],
                                        y_buf[:, g * D:(g + 1) * D],
                                        ALU.mult, ALU.add), "v")
                                    stt_done[g] = pso_free[so]
                                gi2 += FGS
                        for t in range(NT16):
                            c.wait("sy", "v", stt_done[t])
                            c.dma("sy", dst[rows + t * P: rows + (t + 1) * P, :],
                                  y_buf[:, t * D:(t + 1) * D], "st")

        def emit_attn(src, dst):
            NB = NTA // 4
            with ExitStack() as aes:
                qk_all = aes.enter_context(nc.sbuf_tensor("qk_all", [P, NTA * 16], F32))
                ksb = aes.enter_context(nc.sbuf_tensor("ksb", [P, D], F32))
                gamB = aes.enter_context(nc.sbuf_tensor("gamB", [P, D], F32))
                betB = aes.enter_context(nc.sbuf_tensor("betB", [P, D], F32))
                gam_row = aes.enter_context(nc.sbuf_tensor("gam_row", [1, D], F32))
                bet_row = aes.enter_context(nc.sbuf_tensor("bet_row", [1, D], F32))
                ksum_sb = aes.enter_context(nc.sbuf_tensor("ksum_sb", [1, D], F32))
                ks8 = aes.enter_context(nc.sbuf_tensor("ks8", [E, D], F32))
                st1a = aes.enter_context(nc.sbuf_tensor("st1a", [P, NTA], F32))
                st2a = aes.enter_context(nc.sbuf_tensor("st2a", [P, NTA], F32))
                stma = aes.enter_context(nc.sbuf_tensor("stma", [P, NTA], F32))
                stea = aes.enter_context(nc.sbuf_tensor("stea", [P, NTA], F32))
                stva = aes.enter_context(nc.sbuf_tensor("stva", [P, NTA], F32))
                strda = aes.enter_context(nc.sbuf_tensor("strda", [P, NTA], F32))
                stnma = aes.enter_context(nc.sbuf_tensor("stnma", [P, NTA], F32))

                c.dma("sy", gam_row[:], gam_d, "lgb")
                c.dma("sy", bet_row[:], bet_d, "lgb")

                def ln_chain(j0, w, v_s1, a_sq):
                    sl = slice(j0, j0 + w)
                    c.wait("v", "v", v_s1)
                    c.inc(nc.vector.tensor_scalar_mul(
                        stma[:, sl], st1a[:, sl], 1.0 / D), "v")
                    c.wait("v", "a", a_sq)
                    c.inc(nc.vector.tensor_scalar_mul(
                        stea[:, sl], st2a[:, sl], 1.0 / D), "v")
                    c.wait("v", "v")
                    c.inc(nc.vector.tensor_tensor(
                        stva[:, sl], stma[:, sl], stma[:, sl], ALU.mult), "v")
                    c.wait("v", "v")
                    c.inc(nc.vector.tensor_sub(
                        stva[:, sl], stea[:, sl], stva[:, sl]), "v")
                    c.wait("v", "v")
                    c.inc(nc.vector.tensor_scalar_add(
                        stva[:, sl], stva[:, sl], LN_EPS), "v")
                    c.wait("a", "v")
                    c.inc(nc.scalar.activation(
                        stea[:, sl], stva[:, sl], AF.Ln), "a")
                    c.wait("v", "a")
                    c.inc(nc.vector.tensor_scalar_mul(
                        stea[:, sl], stea[:, sl], -0.5), "v")
                    c.wait("a", "v")
                    c.inc(nc.scalar.activation(
                        strda[:, sl], stea[:, sl], AF.Exp), "a")
                    a_rstd = c.cnt["a"]
                    c.wait("v", "a", a_rstd)
                    c.inc(nc.vector.tensor_tensor(
                        stnma[:, sl], stma[:, sl], strda[:, sl], ALU.mult), "v")
                    c.wait("v", "v")
                    c.inc(nc.vector.tensor_scalar_mul(
                        stnma[:, sl], stnma[:, sl], -1.0), "v")
                    return c.cnt["v"]

                with ExitStack() as s1es:
                    wqkvg = [s1es.enter_context(nc.sbuf_tensor(
                        f"w{n}_sb", [P, KD * D], BF16)) for n in "qkvg"]
                    ybuf = s1es.enter_context(nc.sbuf_tensor("a1y", [P, 8 * D], F32))
                    xn_a = [s1es.enter_context(nc.sbuf_tensor(
                        f"a1xn{i}", [P, D], BF16)) for i in range(2)]
                    xnT_a = [s1es.enter_context(nc.sbuf_tensor(
                        f"a1xnT{i}", [P, KD * P], BF16)) for i in range(2)]
                    tmpA = s1es.enter_context(nc.sbuf_tensor("a1tA", [P, D], F32))
                    tmpB = s1es.enter_context(nc.sbuf_tensor("a1tB", [P, D], F32))
                    q1b = [s1es.enter_context(nc.sbuf_tensor(
                        f"a1q{i}", [P, D], F32)) for i in range(2)]
                    vb = [s1es.enter_context(nc.sbuf_tensor(
                        f"a1v{i}", [P, D], F32)) for i in range(2)]
                    gbv = [s1es.enter_context(nc.sbuf_tensor(
                        f"a1g{i}", [P, D], F32)) for i in range(2)]
                    k1b = s1es.enter_context(nc.sbuf_tensor("a1k1", [P, D], F32))
                    k1bf = s1es.enter_context(nc.sbuf_tensor("a1k1b", [P, D], BF16))
                    sqa = s1es.enter_context(nc.sbuf_tensor("a1sq", [P, D], F32))
                    tmpQK = s1es.enter_context(nc.sbuf_tensor("a1qk", [P, D], F32))

                    for w_sb, w_dr in zip(wqkvg, (wq_d, wk_d, wv_d, wg_d)):
                        c.dma("g", w_sb[:].rearrange("p (k n) -> p k n", k=KD),
                              w_dr.rearrange("(k p) n -> p k n", p=P), "wc")
                    w_attn = c.cnt["wc"]

                    with ExitStack() as p1es:
                        ps_tr = [p1es.enter_context(nc.psum_tensor(
                            f"a1tr{i}", [P, P], BF16)) for i in range(2)]
                        ps_pj = [p1es.enter_context(nc.psum_tensor(
                            f"a1pj{i}", [P, D], F32)) for i in range(2)]
                        ps_ks = [p1es.enter_context(nc.psum_tensor(
                            f"a1ks{i}", [1, TQW], F32)) for i in range(2)]
                        c.tr_evac = [None, None]
                        pj_free = {}
                        xn_done = {}
                        spill_st = {}
                        last_proj = {}
                        prev_a_exp = None
                        prev_p_add = None
                        prev_qk_red = None
                        t_ks = None
                        lds = {}

                        def a1_load(b):
                            lsem = f"la{b % 2}"
                            for i_ in range(4):
                                t_ = 4 * b + i_
                                if t_ >= 8:
                                    c.wait("sy", "a", xn_done[t_ - 8])
                                c.dma("sy", ybuf[:, (t_ % 8) * D:(t_ % 8 + 1) * D],
                                      src[t_ * P:(t_ + 1) * P, :], lsem)
                            return lsem, c.cnt[lsem]

                        lds[0] = a1_load(0)
                        lds[1] = a1_load(1)
                        for b in range(NB):
                            lsem, ldc = lds[b]
                            j0 = 4 * b
                            for i_ in range(4):
                                t = j0 + i_
                                yt = ybuf[:, (t % 8) * D:(t % 8 + 1) * D]
                                c.wait("a", lsem, ldc)
                                c.wait("a", "a")
                                c.inc(nc.scalar.activation(
                                    sqa[:], yt, AF.Square,
                                    accum_out=st2a[:, t:t + 1]), "a")
                                c.wait("v", lsem, ldc)
                                c.inc(nc.vector.reduce_sum(
                                    st1a[:, t:t + 1], yt, AX.X), "v")
                            v_nm = ln_chain(j0, 4, c.cnt["v"], c.cnt["a"])
                            for i_ in range(4):
                                t = j0 + i_
                                tb = t % 2
                                c.wait("a", "v", v_nm)
                                if last_proj.get(t - 2) is not None:
                                    c.wait("a", "t", last_proj[t - 2])
                                a_xn = c.inc(nc.scalar.activation(
                                    xn_a[tb][:], ybuf[:, (t % 8) * D:(t % 8 + 1) * D],
                                    AF.Identity, bias=stnma[:, t:t + 1],
                                    scale=strda[:, t:t + 1]), "a")
                                xn_done[t] = a_xn
                                evac_cnt = {}

                                def emit_tr(j):
                                    slot = j % 2
                                    if c.tr_evac[slot] is not None:
                                        c.wait("t", "a", c.tr_evac[slot])
                                    c.wait("t", "a", a_xn)
                                    mm = c.inc(nc.tensor.transpose(
                                        ps_tr[slot][:],
                                        xn_a[tb][:, j * P:(j + 1) * P],
                                        ident_bf[:]), "t")
                                    c.wait("a", "t", mm)
                                    c.tr_evac[slot] = c.inc(nc.scalar.copy(
                                        xnT_a[tb][:, j * P:(j + 1) * P],
                                        ps_tr[slot][:]), "a")
                                    evac_cnt[j] = c.tr_evac[slot]

                                c.wait("t", "wc", w_attn)
                                emit_tr(0)
                                emit_tr(1)
                                v_kbf = None
                                for pi, w_sb in enumerate(wqkvg):
                                    so = pi % 2
                                    if pj_free.get(so) is not None:
                                        c.wait("t", "v", pj_free[so][0])
                                        c.wait("t", "a", pj_free[so][1])
                                    if pi >= 1:
                                        c.wait("t", "a", evac_cnt[KD - 1])
                                    for k in range(KD):
                                        if pi == 0:
                                            c.wait("t", "a", evac_cnt[k])
                                        for half in range(2):
                                            ins = nc.tensor.matmul(
                                                ps_pj[so][:, half * TQW:(half + 1) * TQW],
                                                xnT_a[tb][:, k * P:(k + 1) * P],
                                                w_sb[:, k * D + half * TQW:
                                                     k * D + (half + 1) * TQW],
                                                start=(k == 0), stop=(k == KD - 1))
                                        if pi == 0 and k + 2 < KD:
                                            emit_tr(k + 2)
                                    t_pj = c.inc(ins, "t")
                                    if pi == 3:
                                        last_proj[t] = t_pj
                                    if pi in (0, 1):
                                        dst_b = q1b[tb] if pi == 0 else k1b
                                        c.wait("v", "t", t_pj)
                                        if prev_a_exp is not None:
                                            c.wait("v", "a", prev_a_exp)
                                        v_min = c.inc(nc.vector.tensor_scalar_min(
                                            tmpA[:], ps_pj[so][:], 0.0), "v")
                                        c.wait("a", "v", v_min)
                                        if prev_p_add is not None:
                                            c.wait("a", "p", prev_p_add)
                                        a_exp = c.inc(nc.scalar.activation(
                                            tmpB[:], tmpA[:], AF.Exp), "a")
                                        prev_a_exp = a_exp
                                        c.wait("a", "t", t_pj)
                                        if pi == 0 and t >= 2:
                                            c.wait("a", *spill_st[t - 2])
                                        a_rel = c.inc(nc.scalar.activation(
                                            dst_b[:], ps_pj[so][:], AF.Relu), "a")
                                        c.wait("g", "a", a_rel)
                                        p_add = c.inc(nc.gpsimd.tensor_add(
                                            dst_b[:], tmpB[:], dst_b[:]), "p")
                                        prev_p_add = p_add
                                        pj_free[so] = (v_min, a_rel)
                                        if pi == 1:
                                            c.wait("v", "p", p_add)
                                            v_kbf = c.inc(nc.vector.tensor_copy(
                                                k1bf[:], k1b[:]), "v")
                                    else:
                                        dst_b = vb[tb] if pi == 2 else gbv[tb]
                                        c.wait("a", "t", t_pj)
                                        if t >= 2:
                                            c.wait("a", *spill_st[t - 2])
                                        a_cp = c.inc(nc.scalar.activation(
                                            dst_b[:], ps_pj[so][:], AF.Copy), "a")
                                        pj_free[so] = (c.cnt["v"], a_cp)
                                if prev_qk_red is not None:
                                    c.wait("g", "v", prev_qk_red)
                                p_qk = c.inc(nc.gpsimd.tensor_mul(
                                    tmpQK[:], q1b[tb][:], k1b[:]), "p")
                                c.wait("v", "p", p_qk)
                                prev_qk_red = c.inc(nc.vector.tensor_reduce(
                                    qk_all[:, t * 16:(t + 1) * 16],
                                    tmpQK[:].rearrange("p (h d) -> p h d", h=16),
                                    AX.X, ALU.add), "v")
                                c.wait("t", "v", v_kbf)
                                for half in range(2):
                                    ins = nc.tensor.matmul(
                                        ps_ks[half][:], ones_c_bf[:],
                                        k1bf[:, half * TQW:(half + 1) * TQW],
                                        start=(t == 0), stop=(t == NTA - 1))
                                if t == NTA - 1:
                                    t_ks = c.inc(ins, "t")
                                c.wait("sy", "v")
                                c.wait("sy", "a")
                                c.wait("sy", "p")
                                ssem = f"sp{tb}"
                                for buf, dr in ((q1b[tb], q1_sp), (vb[tb], v_sp),
                                                (gbv[tb], g_sp)):
                                    c.dma("sy", dr[t * P:(t + 1) * P, :], buf[:], ssem)
                                spill_st[t] = (ssem, c.cnt[ssem])
                            if b + 2 < NB:
                                lds[b + 2] = a1_load(b + 2)
                        c.wait("v", "t", t_ks)
                        c.inc(nc.vector.tensor_copy(
                            ksum_sb[0:1, 0:TQW], ps_ks[0][:]), "v")
                        c.wait("v", "v")
                        c.inc(nc.vector.tensor_copy(
                            ksum_sb[0:1, TQW:D], ps_ks[1][:]), "v")
                        c.wait("sy", "v")
                        c.dma("sy", cc_in, ksum_sb[:], "st")
                        c.wait("g", "st")
                        c.inc(nc.gpsimd.collective_compute(
                            "AllGather", ALU.bypass,
                            replica_groups=[list(range(NCORES))],
                            ins=[cc_in], outs=[cc_out]), "cc")
                        c.wait("sy", "cc")
                        c.dma("sy", ks8[:], cc_out, "lgb")
                with ExitStack() as kes:
                    ps_kb = kes.enter_context(nc.psum_tensor("ps_kb", [P, D], F32))
                    ps_gb = kes.enter_context(nc.psum_tensor("ps_gb", [P, D], F32))
                    c.wait("t", "v")
                    c.wait("t", "a")
                    c.wait("t", "lgb", 3)
                    for half in range(2):
                        ins = nc.tensor.matmul(
                            ps_kb[:, half * TQW:(half + 1) * TQW], sel_sb[:],
                            ks8[0:E, half * TQW:(half + 1) * TQW],
                            start=True, stop=True)
                    t_kb = c.inc(ins, "t")
                    c.wait("v", "t", t_kb)
                    c.inc(nc.vector.tensor_copy(ksb[:], ps_kb[:]), "v")
                    for half in range(2):
                        ins = nc.tensor.matmul(
                            ps_gb[:, half * TQW:(half + 1) * TQW], ones_r_f[:],
                            gam_row[0:1, half * TQW:(half + 1) * TQW],
                            start=True, stop=True)
                    t_g = c.inc(ins, "t")
                    c.wait("v", "t", t_g)
                    v_gb = c.inc(nc.vector.tensor_copy(gamB[:], ps_gb[:]), "v")
                    c.wait("t", "v", v_gb)
                    for half in range(2):
                        ins = nc.tensor.matmul(
                            ps_gb[:, half * TQW:(half + 1) * TQW], ones_r_f[:],
                            bet_row[0:1, half * TQW:(half + 1) * TQW],
                            start=True, stop=True)
                    t_b = c.inc(ins, "t")
                    c.wait("v", "t", t_b)
                    c.inc(nc.vector.tensor_copy(betB[:], ps_gb[:]), "v")
                    v_kgb = c.cnt["v"]

                with ExitStack() as s2es:
                    wo_sb = s2es.enter_context(nc.sbuf_tensor("wo_sb", [P, KD * D], BF16))
                    q1l = s2es.enter_context(nc.sbuf_tensor("a2q", [P, 4 * D], F32))
                    vl = s2es.enter_context(nc.sbuf_tensor("a2v", [P, 4 * D], F32))
                    gl = s2es.enter_context(nc.sbuf_tensor("a2g", [P, 4 * D], F32))
                    yl = s2es.enter_context(nc.sbuf_tensor("a2yl", [P, 4 * D], F32))
                    ztb = s2es.enter_context(nc.sbuf_tensor("a2zt", [P, 8 * D], F32))
                    sqz = s2es.enter_context(nc.sbuf_tensor("a2sq", [P, D], F32))
                    sqz2 = s2es.enter_context(nc.sbuf_tensor("a2sq2", [P, D], F32))
                    nrm_all = s2es.enter_context(nc.sbuf_tensor("a2nr", [P, NTA * 16], F32))
                    rq_all = s2es.enter_context(nc.sbuf_tensor("a2rq", [P, NTA * 16], F32))
                    zn = [s2es.enter_context(nc.sbuf_tensor(
                        f"a2zn{i}", [P, D], F32)) for i in range(2)]
                    sil = [s2es.enter_context(nc.sbuf_tensor(
                        f"a2sl{i}", [P, D], F32)) for i in range(2)]
                    zs = [s2es.enter_context(nc.sbuf_tensor(
                        f"a2zs{i}", [P, D], BF16)) for i in range(2)]
                    zT = [s2es.enter_context(nc.sbuf_tensor(
                        f"a2zT{i}", [P, KD * P], BF16)) for i in range(2)]
                    yo = [s2es.enter_context(nc.sbuf_tensor(
                        f"a2yo{i}", [P, D], F32)) for i in range(2)]
                    c.wait("g", "v")
                    c.wait("g", "a")
                    c.wait("sy", "t")
                    c.wait("sy", "v")
                    c.wait("sy", "a")
                    c.wait("sy", "p")
                    c.wait("sy", "st")
                    c.wait("sy", "sp0")
                    c.wait("sy", "sp1")
                    c.dma("g", wo_sb[:].rearrange("p (k n) -> p k n", k=KD),
                          wo_d.rearrange("(k p) n -> p k n", p=P), "wc")
                    w_wo = c.cnt["wc"]
                    with ExitStack() as p3es:
                        ps_tr = [p3es.enter_context(nc.psum_tensor(
                            f"a3tr{i}", [P, P], BF16)) for i in range(2)]
                        ps_ao = [p3es.enter_context(nc.psum_tensor(
                            f"a3o{i}", [P, D], F32)) for i in range(2)]
                        c.wait("t", "v")
                        c.wait("t", "a")
                        c.tr_evac = [None, None]
                        use_qv = {}
                        sil_done = {}
                        yo_done = {}
                        ao_free = {}
                        ao_mm = {}
                        zs_read = {}
                        y2st = {}
                        v_zs = {}
                        a_zn_by_t = {}
                        ldq = {}
                        ldv = {}
                        ld2 = {}
                        p_bet = {}
                        use_q = {}
                        use_v = {}
                        prev_nrm = None
                        pending_yo = []

                        def a2_ldq(b):
                            lsem = f"lv{b % 2}"
                            for i_ in range(4):
                                t_ = 4 * b + i_
                                if t_ >= 4:
                                    c.wait("sy", "p", use_q[t_ - 4])
                                c.dma("sy", q1l[:, (t_ % 4) * D:(t_ % 4 + 1) * D],
                                      q1_sp[t_ * P:(t_ + 1) * P, :], lsem)
                                ldq[t_] = (lsem, c.cnt[lsem])

                        def a2_ldv(b):
                            lsem = f"lv{b % 2}"
                            for i_ in range(4):
                                t_ = 4 * b + i_
                                if t_ >= 4:
                                    c.wait("sy", "p", use_v[t_ - 4])
                                c.dma("sy", vl[:, (t_ % 4) * D:(t_ % 4 + 1) * D],
                                      v_sp[t_ * P:(t_ + 1) * P, :], lsem)
                                ldv[t_] = (lsem, c.cnt[lsem])

                        def a2_ld2(t_):
                            lsem = f"lb{t_ % 2}"
                            if t_ >= 4:
                                c.wait("sy", "a", sil_done[t_ - 4])
                                c.wait("sy", "v", yo_done[t_ - 4])
                            c.dma("sy", gl[:, (t_ % 4) * D:(t_ % 4 + 1) * D],
                                  g_sp[t_ * P:(t_ + 1) * P, :], lsem)
                            c.dma("sy", yl[:, (t_ % 4) * D:(t_ % 4 + 1) * D],
                                  src[t_ * P:(t_ + 1) * P, :], lsem)
                            ld2[t_] = (lsem, c.cnt[lsem])

                        def flush_yo(keep=0):
                            while len(pending_yo) > keep:
                                t_, t_ao_ = pending_yo.pop(0)
                                tb_ = t_ % 2
                                c.wait("v", "t", t_ao_)
                                c.wait("v", *ld2[t_])
                                if t_ >= 2:
                                    c.wait("v", *y2st[t_ - 2])
                                v_out = c.inc(nc.vector.tensor_add(
                                    yo[tb_][:], ps_ao[tb_][:],
                                    yl[:, (t_ % 4) * D:(t_ % 4 + 1) * D]), "v")
                                yo_done[t_] = v_out
                                ao_free[tb_] = v_out
                                c.wait("sy", "v", v_out)
                                c.dma("sy", dst[t_ * P:(t_ + 1) * P, :],
                                      yo[tb_][:], f"sp{tb_}")
                                y2st[t_] = (f"sp{tb_}", c.cnt[f"sp{tb_}"])
                                if t_ + 4 < NTA:
                                    a2_ld2(t_ + 4)

                        a2_ldq(0)
                        a2_ldv(0)
                        for t_ in range(min(4, NTA)):
                            a2_ld2(t_)
                        v_nmz_by_b = {}
                        v_rqv_by_b = {}

                        def emit_s1(b):
                            nonlocal prev_nrm
                            j0 = 4 * b
                            for i_ in range(4):
                                t = j0 + i_
                                c.wait("g", *ldq[t])
                                c.wait("g", "v", v_kgb)
                                if prev_nrm is not None:
                                    c.wait("g", "v", prev_nrm)
                                p_sqz = c.inc(nc.gpsimd.tensor_mul(
                                    sqz[:], q1l[:, (t % 4) * D:(t % 4 + 1) * D],
                                    ksb[:]), "p")
                                use_q[t] = c.cnt["p"]
                                c.wait("v", "p", p_sqz)
                                prev_nrm = c.inc(nc.vector.tensor_reduce(
                                    nrm_all[:, t * 16:(t + 1) * 16],
                                    sqz[:].rearrange("p (h d) -> p h d", h=16),
                                    AX.X, ALU.add), "v")
                            cs = slice(j0 * 16, (j0 + 4) * 16)
                            c.wait("v", "v")
                            c.inc(nc.vector.tensor_scalar_add(
                                nrm_all[:, cs], nrm_all[:, cs], NORM_EPS), "v")
                            c.wait("v", "v")
                            c.inc(nc.vector.reciprocal(
                                rq_all[:, cs], nrm_all[:, cs]), "v")
                            c.wait("v", "v")
                            c.inc(nc.vector.tensor_tensor(
                                rq_all[:, cs], qk_all[:, cs], rq_all[:, cs],
                                ALU.mult), "v")
                            v_rqv_by_b[b] = c.cnt["v"]

                        def emit_s2(b):
                            j0 = 4 * b
                            zb = b % 2
                            if b >= 2:
                                c.wait("g", "a", a_zn_by_t[4 * (b - 2) + 3])
                            for i_ in range(4):
                                t = j0 + i_
                                zt_i = ztb[:, (zb * 4 + i_) * D:(zb * 4 + i_ + 1) * D]
                                c.wait("g", *ldv[t])
                                c.wait("g", "v", v_rqv_by_b[b])
                                c.inc(nc.gpsimd.tensor_tensor(
                                    zt_i.rearrange("p (h d) -> p h d", h=16),
                                    vl[:, (t % 4) * D:(t % 4 + 1) * D]
                                    .rearrange("p (h d) -> p h d", h=16),
                                    rq_all[:, t * 16:(t + 1) * 16]
                                    .unsqueeze(2).broadcast_to([P, 16, 64]),
                                    ALU.mult), "p")
                                use_v[t] = c.cnt["p"]
                                c.wait("a", "p", c.cnt["p"])
                                c.wait("a", "a")
                                c.inc(nc.scalar.activation(
                                    sqz2[:], zt_i, AF.Square,
                                    accum_out=st2a[:, t:t + 1]), "a")
                                c.wait("v", "p", c.cnt["p"])
                                c.wait("v", "v")
                                c.inc(nc.vector.reduce_sum(
                                    st1a[:, t:t + 1], zt_i, AX.X), "v")
                            v_nmz_by_b[b] = ln_chain(j0, 4, c.cnt["v"], c.cnt["a"])

                        def emit_stage3(b):
                            j0 = 4 * b
                            zb = b % 2
                            v_nmz = v_nmz_by_b[b]
                            for i_ in range(4):
                                t = j0 + i_
                                tb = t % 2
                                flush_yo(keep=1)
                                zt_i = ztb[:, (zb * 4 + i_) * D:(zb * 4 + i_ + 1) * D]
                                c.wait("a", "v", v_nmz)
                                if t >= 2:
                                    c.wait("a", "v", v_zs[t - 2])
                                    c.wait("a", "p", p_bet[t - 2])
                                a_zn = c.inc(nc.scalar.activation(
                                    zn[tb][:], zt_i, AF.Identity,
                                    bias=stnma[:, t:t + 1],
                                    scale=strda[:, t:t + 1]), "a")
                                a_zn_by_t[t] = a_zn
                                c.wait("a", *ld2[t])
                                a_sil = c.inc(nc.scalar.activation(
                                    sil[tb][:], gl[:, (t % 4) * D:(t % 4 + 1) * D],
                                    AF.Silu), "a")
                                sil_done[t] = a_sil
                                c.wait("g", "a", a_sil)
                                c.wait("g", "v", v_kgb)
                                c.inc(nc.gpsimd.tensor_mul(
                                    zn[tb][:], zn[tb][:], gamB[:]), "p")
                                c.wait("g", "p")
                                p_bet[t] = c.inc(nc.gpsimd.tensor_add(
                                    zn[tb][:], zn[tb][:], betB[:]), "p")
                                c.wait("v", "p", p_bet[t])
                                c.wait("v", "v")
                                if t >= 2:
                                    c.wait("v", "t", zs_read[t - 2])
                                v_zs[t] = c.inc(nc.vector.tensor_tensor(
                                    zs[tb][:], zn[tb][:], sil[tb][:],
                                    ALU.mult), "v")
                                evac2 = {}

                                def emit_tr2(j):
                                    slot = j % 2
                                    if c.tr_evac[slot] is not None:
                                        c.wait("t", "a", c.tr_evac[slot])
                                    c.wait("t", "v", v_zs[t])
                                    mm = c.inc(nc.tensor.transpose(
                                        ps_tr[slot][:],
                                        zs[tb][:, j * P:(j + 1) * P],
                                        ident_bf[:]), "t")
                                    c.wait("a", "t", mm)
                                    if ao_mm.get(t - 2) is not None:
                                        c.wait("a", "t", ao_mm[t - 2])
                                    c.tr_evac[slot] = c.inc(nc.scalar.copy(
                                        zT[tb][:, j * P:(j + 1) * P],
                                        ps_tr[slot][:]), "a")
                                    evac2[j] = c.tr_evac[slot]
                                    return mm

                                emit_tr2(0)
                                emit_tr2(1)
                                if ao_free.get(tb) is not None:
                                    c.wait("t", "v", ao_free[tb])
                                c.wait("t", "wc", w_wo)
                                last_tr = None
                                for k in range(KD):
                                    c.wait("t", "a", evac2[k])
                                    for half in range(2):
                                        ins = nc.tensor.matmul(
                                            ps_ao[tb][:, half * TQW:(half + 1) * TQW],
                                            zT[tb][:, k * P:(k + 1) * P],
                                            wo_sb[:, k * D + half * TQW:
                                                  k * D + (half + 1) * TQW],
                                            start=(k == 0), stop=(k == KD - 1))
                                    if k + 2 < KD:
                                        last_tr = emit_tr2(k + 2)
                                zs_read[t] = last_tr
                                t_ao = c.inc(ins, "t")
                                ao_mm[t] = t_ao
                                pending_yo.append((t, t_ao))

                        for k in range(NB + 2):
                            if k >= 2:
                                emit_stage3(k - 2)
                            if k < NB:
                                emit_s1(k)
                                if k + 1 < NB:
                                    a2_ldq(k + 1)
                            if 0 <= k - 1 < NB:
                                emit_s2(k - 1)
                                if k < NB:
                                    a2_ldv(k)
                        flush_yo()

        emit_ffn(x_in, y1_d, "f1")
        for ek in ("t", "v", "a", "sy", "g"):
            c.wait(ek, "t")
            c.wait(ek, "v")
            c.wait(ek, "a")
            c.wait(ek, "st")
            c.wait(ek, "ly")
        emit_attn(y1_d, y2_d)
        for ek in ("t", "v", "a", "sy", "g"):
            c.wait(ek, "t")
            c.wait(ek, "v")
            c.wait(ek, "a")
            c.wait(ek, "p")
            c.wait(ek, "st")
            c.wait(ek, "sp0")
            c.wait(ek, "sp1")
        emit_ffn(y2_d, out_d, "f2")
        c.wait("sy", "st")
    return nc


_NC_CACHE = {}


def _get_nc(T, TP):
    key = (T, TP)
    if key not in _NC_CACHE:
        _NC_CACHE[key] = build(T, TP)
    return _NC_CACHE[key]


def kernel(L0_attn_hidden_states, L0_attn_Wq, L0_attn_Wk, L0_attn_Wv, L0_attn_Wg,
           L0_attn_Wo, L0_attn_gamma, L0_attn_beta, L0_ffn_Wr, L0_ffn_br,
           L0_ffn_W1, L0_ffn_b1, L0_ffn_W2, L0_ffn_b2):
    x = np.ascontiguousarray(np.asarray(L0_attn_hidden_states, dtype=np.float32))
    B, S, H = x.shape
    xf = x.reshape(B * S, H)
    T = (B * S) // NCORES
    TP = 2048 if T % 2048 == 0 else T
    nc = _get_nc(T, TP)

    ident_bf = np.eye(P, dtype=ml_dtypes.bfloat16)
    ident_f = np.eye(P, dtype=np.float32)
    ones_r_bf = np.ones((1, P), dtype=ml_dtypes.bfloat16)
    ones_r_f = np.ones((1, P), dtype=np.float32)
    ones_c_bf = np.ones((P, 1), dtype=ml_dtypes.bfloat16)

    common = dict(
        wq=np.ascontiguousarray(L0_attn_Wq, dtype=np.float32),
        wk=np.ascontiguousarray(L0_attn_Wk, dtype=np.float32),
        wv=np.ascontiguousarray(L0_attn_Wv, dtype=np.float32),
        wg=np.ascontiguousarray(L0_attn_Wg, dtype=np.float32),
        wo=np.ascontiguousarray(L0_attn_Wo, dtype=np.float32),
        gam=np.ascontiguousarray(
            np.asarray(L0_attn_gamma, dtype=np.float32).reshape(1, H)),
        bet=np.ascontiguousarray(
            np.asarray(L0_attn_beta, dtype=np.float32).reshape(1, H)),
        wr=np.ascontiguousarray(L0_ffn_Wr, dtype=np.float32),
        br=np.ascontiguousarray(
            np.asarray(L0_ffn_br, dtype=np.float32).reshape(1, E)),
        w1=np.ascontiguousarray(L0_ffn_W1, dtype=np.float32),
        b1=np.ascontiguousarray(L0_ffn_b1, dtype=np.float32),
        w2=np.ascontiguousarray(L0_ffn_W2, dtype=np.float32),
        b2=np.ascontiguousarray(L0_ffn_b2, dtype=np.float32),
        ident_bf=ident_bf, ident_f=ident_f, ones_r_bf=ones_r_bf,
        ones_r_f=ones_r_f, ones_c_bf=ones_c_bf,
    )
    in_maps = []
    for cid in range(NCORES):
        sel = np.zeros((E, P), dtype=np.float32)
        b = cid // 2
        sel[2 * b, :] = 1.0
        sel[2 * b + 1, :] = 1.0
        im = dict(common)
        im["sel"] = sel
        im["x"] = np.ascontiguousarray(xf[cid * T:(cid + 1) * T])
        in_maps.append(im)

    res = run_bass_kernel_spmd(nc, in_maps, core_ids=list(range(NCORES)))
    out = np.concatenate([r["out"] for r in res.results], axis=0)
    return out.reshape(B, S, H)
